# version 04
# baseline (speedup 1.0000x reference)
import numpy as np
import ml_dtypes
import concourse.bass as bass
import concourse.mybir as mybir
from concourse.bass_utils import run_bass_kernel_spmd

F32 = mybir.dt.float32
BF16 = mybir.dt.bfloat16
I32 = mybir.dt.int32
AF = mybir.ActivationFunctionType
ALU = mybir.AluOpType
AX = mybir.AxisListType
BF16NP = ml_dtypes.bfloat16

T_LOC = 4096
D = 1024
NH = 16
NCHUNK = 32
NGROUP = 8
E = 8
CAP = 5120
NTC = CAP // 128
I_R = 4096
I_S = 8192
EPS_LN = 1e-5
EPS_RMS = float(np.finfo(np.float32).eps)
BIG = 1.0e9

ENGS = ("pe", "dve", "act", "pool", "sp")
DMAQ = ("sp", "act", "pool")


class Emitter:

    def __init__(self, nc):
        self.nc = nc
        self.ops = {e: [] for e in ENGS}
        self.last_writer = {}
        self.readers = {}
        self.dsem = {}
        self.cc_n = 0
        self.esem = {e: nc.semaphore(f"esem_{e}").__enter__() for e in ENGS}
        self.ccsem = nc.semaphore("ccsem").__enter__()
        self.barsem = nc.semaphore("barsem").__enter__()
        self.bar_n = 0

    def _dep(self, waits, ev):
        if ev is None:
            return
        if ev[0] == "cc":
            key, val = ("cc",), ev[1]
        else:
            key, val = (ev[0], ev[1]), ev[2]
        if waits.get(key, -1) < val:
            waits[key] = val

    def emit(self, eng, fn, reads=(), writes=(), dma=False, cc=False, tag=""):
        waits = {}
        for r in reads:
            self._dep(waits, self.last_writer.get(r))
        for w in writes:
            self._dep(waits, self.last_writer.get(w))
            for ev in self.readers.get(w, ()):
                self._dep(waits, ev)
        idx = len(self.ops[eng])
        sem_key = None
        if dma:
            assert eng in DMAQ and len(writes) >= 1
            sem_key = writes[0]
            if sem_key not in self.dsem:
                self.dsem[sem_key] = [
                    self.nc.semaphore(f"ds_{len(self.dsem)}").__enter__(), 0, eng]
            rec = self.dsem[sem_key]
            rec[1] += 16
            rec[2] = eng
            ev = ("dma", sem_key, rec[1])
        elif cc:
            self.cc_n += 1
            ev = ("cc", self.cc_n)
        else:
            ev = ("eng", eng, idx)
        if eng == "pe":
            waits = {k: v for k, v in waits.items()
                     if not (k[0] == "eng" and k[1] == "pe")}
        self.ops[eng].append(dict(fn=fn, waits=waits, dma=dma, cc=cc,
                                  tag=tag, need_inc=False, is_bar=False,
                                  sem_key=sem_key))
        for r in reads:
            self.readers.setdefault(r, []).append(ev)
        for w in writes:
            self.last_writer[w] = ev
            self.readers[w] = []
        return ev

    def drain(self, q):
        pend = [(rec[0], rec[1]) for rec in self.dsem.values() if rec[2] == q]

        def fn(h):
            for sem, n in pend:
                h.wait_ge(sem, n)
            return h.nop()
        self.ops[q].append(dict(fn=fn, waits={}, dma=False, cc=False,
                                tag=f"drain_{q}", need_inc=False, is_bar=False,
                                sem_key=None))

    def barrier(self):
        self.bar_n += 1
        target = 5 * self.bar_n
        pend = {q: [(rec[0], rec[1]) for rec in self.dsem.values()
                    if rec[2] == q] for q in DMAQ}

        def mk(eng):
            def fn(h):
                for sem, n in pend.get(eng, ()):
                    h.wait_ge(sem, n)
                h.nop().then_inc(self.barsem, 1)
                h.wait_ge(self.barsem, target)
            return fn
        for e in ENGS:
            self.ops[e].append(dict(fn=mk(e), waits={}, dma=False, cc=False,
                                    tag=f"bar{self.bar_n}", need_inc=False,
                                    is_bar=True, sem_key=None))
        self.last_writer = {k: v for k, v in self.last_writer.items()
                            if v[0] == "cc"}
        self.readers = {}

    def finalize(self):
        for e in ENGS:
            for op in self.ops[e]:
                for key, val in op["waits"].items():
                    if key[0] == "eng":
                        self.ops[key[1]][val]["need_inc"] = True
        for e in ENGS:
            c = 0
            for op in self.ops[e]:
                if op["need_inc"]:
                    c += 1
                op["inc_no"] = c

    def replay(self, eng, h):
        waited = {}
        for op in self.ops[eng]:
            if op["is_bar"]:
                op["fn"](h)
                waited = {}
                continue
            for key, val in sorted(op["waits"].items(), key=lambda kv: str(kv[0])):
                if key[0] == "eng":
                    sem = self.esem[key[1]]
                    target = self.ops[key[1]][val]["inc_no"]
                elif key[0] == "dma":
                    sem, target = self.dsem[key[1]][0], val
                else:
                    sem, target = self.ccsem, val
                if waited.get(key, -1) < target:
                    h.wait_ge(sem, target)
                    waited[key] = target
            ins = op["fn"](h)
            if op["dma"]:
                ins.then_inc(self.dsem[op["sem_key"]][0], 16)
            elif op["cc"]:
                ins.then_inc(self.ccsem, 1)
            elif op["need_inc"]:
                assert ins is not None, f"op {op['tag']} needs inc, returned None"
                ins.then_inc(self.esem[eng], 1)


def build_program(debug=False):
    nc = bass.Bass("TRN2", target_bir_lowering=False)

    def din(name, shape, dt=F32):
        return nc.dram_tensor(name, shape, dt, kind="ExternalInput").ap()

    def dout(name, shape, dt=F32):
        return nc.dram_tensor(name, shape, dt, kind="ExternalOutput").ap()

    def dint(name, shape, dt):
        return nc.dram_tensor(name, shape, dt, kind="Internal").ap()

    x_in = din("x", [T_LOC, D])
    W1 = din("W1", [D, 3 * D])
    b1c = din("b1c", [128, 24])
    b1kv = din("b1kv", [128, 2 * D])
    W2 = din("W2", [D, D])
    b2bc = din("b2bc", [128, D])
    rwT = din("rwT", [D, E])
    rg = din("rg", [D, I_R])
    ru = din("ru", [D, I_R])
    rd = din("rd", [I_R, D])
    sg = din("sg", [D, I_S])
    su = din("su", [D, I_S])
    sd = din("sd", [I_S, D])
    identb = din("identb", [128, 128], BF16)
    identf = din("identf", [128, 128])
    utmask = din("utmask", [128, 128])
    ltstrict = din("ltstrict", [128, 128])
    lt8 = din("lt8", [128, 128])
    cs8 = din("cs8", [128, 128])
    iota_f128 = din("iota_f128", [128, 128])
    iota_f40 = din("iota_f40", [128, NTC])
    iota_p = din("iota_p", [128, 1])
    tok_pg = din("tok_pg", [128, 256])
    ones_col = din("ones_col", [128, 1])
    ones_row = din("ones_row", [1, 128])
    csel = din("csel", [128, E])
    prevmask = din("prevmask", [E, 1])
    y_out = dout("y", [T_LOC, D])

    rgb = dint("rgb", [I_R // 128, 128, 8, 128], BF16)
    rub = dint("rub", [I_R // 128, 128, 8, 128], BF16)
    rdb = dint("rdb", [I_R, D], BF16)
    sgb = dint("sgb", [I_S // 128, 128, 8, 128], BF16)
    sub = dint("sub", [I_S // 128, 128, 8, 128], BF16)
    sdb = dint("sdb", [I_S, D], BF16)
    x1d = dint("x1d", [T_LOC, D], F32)
    xfd = dint("xfd", [T_LOC, D], BF16)
    lgd = dint("lgd", [129, 256], F32)
    ctd = dint("ctd", [CAP, D], BF16)
    shd = dint("shd", [T_LOC, D], F32)
    xf_all = dint("xf_all", [8 * T_LOC, D], BF16)
    lg_all = dint("lg_all", [8 * 129, 256], F32)
    ct_all = dint("ct_all", [8 * CAP, D], BF16)

    if debug:
        dbg = {nm: dout("dbg_" + nm, shp, dt) for nm, shp, dt in (
            ("x1", [T_LOC, D], F32), ("xf", [T_LOC, D], BF16),
            ("lg", [129, 256], F32), ("disp", [128, NTC], I32),
            ("wv", [128, NTC], F32), ("ct", [CAP, D], BF16),
            ("sh", [T_LOC, D], F32), ("ro", [128, NCHUNK * 2], I32),
            ("rnk", [128, 256], F32), ("sel", [128, 256], F32),
            ("wdc", [128, 256], F32), ("xln", [T_LOC, D], F32),
            ("ktm", [T_LOC, D], BF16), ("vtm", [T_LOC, D], BF16),
            ("kq", [NGROUP, 16, 128, 512], BF16),
            ("am", [NGROUP, 8, 128, 512], BF16))}

    em = Emitter(nc)
    regs = {}
    guards = []

    def sb(name, shape, dt):
        g = nc.sbuf_tensor(name, shape, dt)
        h = g.__enter__()
        guards.append((name, g))
        return h

    def sb_free(*names):
        want = set(names)
        while want:
            n, g = guards[-1]
            assert n in want, f"sb_free non-tail: {n} not in requested set"
            g.__exit__(None, None, None)
            guards.pop()
            want.discard(n)

    psf = [nc.psum_tensor(f"psf{i}", [128, 512], F32).__enter__() for i in range(7)]
    psb = nc.psum_tensor("psb", [128, 1024], BF16).__enter__()
    BKB = "bank_bf"

    cst = {}
    for nm, ap_, shape, dt in (
        ("identb", identb, [128, 128], BF16), ("identf", identf, [128, 128], F32),
        ("utmask", utmask, [128, 128], F32), ("ltstrict", ltstrict, [128, 128], F32),
        ("lt8", lt8, [128, 128], F32), ("cs8", cs8, [128, 128], F32),
        ("iota_f128", iota_f128, [128, 128], F32), ("iota_f40", iota_f40, [128, NTC], F32),
        ("iota_p", iota_p, [128, 1], F32), ("tok_pg", tok_pg, [128, 256], F32),
        ("ones_col", ones_col, [128, 1], F32),
        ("ones_row", ones_row, [1, 128], F32), ("csel", csel, [128, E], F32),
        ("prevmask", prevmask, [E, 1], F32), ("b1c", b1c, [128, 24], F32),
        ("b1kv", b1kv, [128, 2 * D], F32), ("b2bc", b2bc, [128, D], F32),
    ):
        cst[nm] = sb("c_" + nm, shape, dt)
        em.emit("sp", (lambda a=ap_, s=cst[nm]: lambda h: h.dma_start(out=s[:], in_=a))(),
                writes=(nm,), dma=True, tag=f"ld_{nm}")
    rwb = sb("rwb", [128, 8 * E], BF16)
    em.emit("pool", lambda h: h.dma_start(
        out=rwb[:].rearrange("p (dk e) -> p dk e", e=E),
        in_=rwT.rearrange("(dk p) e -> p dk e", p=128)),
        writes=("rwb",), dma=True, tag="ld_rwb")

    lg_sb = sb("lg_sb", [128, 256], F32)
    eq1L = sb("eq1L", [128, 256], F32)
    eq2L = sb("eq2L", [128, 256], F32)
    rnkL = sb("rnkL", [128, 256], F32)
    retoff_i = sb("retoff_i", [128, NCHUNK * 2], I32)
    disp_i = sb("disp_i", [128, NTC], I32)
    wv_sb = sb("wv_sb", [128, NTC], F32)
    sc1 = sb("sc1", [128, 256], F32)
    sc2 = sb("sc2", [128, 256], F32)
    sc3 = sb("sc3", [128, 256], F32)
    sc4 = sb("sc4", [128, 256], F32)
    srow = sb("srow", [128, 16], F32)

    W1b = [sb(f"W1b{k}", [128, 3 * D], BF16) for k in range(8)]
    W2b = [sb(f"W2b{k}", [128, D], BF16) for k in range(8)]
    for k in range(8):
        em.emit("pool", (lambda k=k: lambda h: h.dma_start(
            out=W1b[k][:], in_=W1[k * 128:(k + 1) * 128, :]))(),
            writes=(f"W1b{k}",), dma=True, tag=f"ld_W1b{k}")
        em.emit("pool", (lambda k=k: lambda h: h.dma_start(
            out=W2b[k][:], in_=W2[k * 128:(k + 1) * 128, :]))(),
            writes=(f"W2b{k}",), dma=True, tag=f"ld_W2b{k}")

    em.drain("pool")
    em.emit("pool", lambda h: h.dma_start(out=rdb[:, :], in_=rd[:, :]),
            writes=("rdb",), dma=True, tag="cast_rd")
    for src, dst, nm in ((rg, rgb, "rgb"), (ru, rub, "rub"),
                         (sg, sgb, "sgb"), (su, sub, "sub")):
        for dk in range(8):
            em.emit("pool", (lambda s=src, d=dst, dk=dk: lambda h: h.dma_start(
                out=d[:, :, dk, :],
                in_=s[dk * 128:(dk + 1) * 128, :].rearrange(
                    "p (i f) -> i p f", f=128)))(),
                writes=(nm,), dma=True, tag=f"cast_{nm}{dk}")
    em.emit("pool", lambda h: h.dma_start(out=sdb[:, :], in_=sd[:, :]),
            writes=("sdb",), dma=True, tag="cast_sd")


    x_tm = [sb(f"x_tm{i}", [128, 4 * D], F32) for i in range(2)]
    xln_tm = sb("xln_tm", [128, D], F32)
    fm_xln = [sb(f"fm_xln{k}", [128, 512], BF16) for k in range(8)]
    fm_kq = [sb(f"fm_kq{m}", [128, 512], BF16) for m in range(16)]
    ktm2 = [sb(f"ktm{i}", [128, D], BF16) for i in range(2)]
    vtm2 = [sb(f"vtm{i}", [128, D], BF16) for i in range(2)]
    a_sb2 = [sb(f"a_sb{i}", [128, 128], BF16) for i in range(2)]
    s_f32 = sb("s_f32", [128, 512], F32)
    s_bf = sb("s_bf", [128, 512], BF16)
    fm_am = [sb(f"fm_am{k}", [128, 512], BF16) for k in range(8)]
    x1_tm = sb("x1_tm", [128, D], F32)
    xf_tm = sb("xf_tm", [128, D], BF16)
    xf32 = sb("xf32", [128, D], F32)
    fm_xfc = [sb(f"fm_xfc{k}", [128, 128], BF16) for k in range(8)]
    stat2 = [sb(f"stat{i}", [128, 8], F32) for i in range(2)]
    x2s2 = [sb(f"x2s{i}", [128, D], F32) for i in range(2)]

    em.emit("dve", lambda h: h.memset(s_f32[:], 0.0), writes=("s_f32",), tag="s0")
    em.emit("dve", lambda h: h.memset(s_bf[:], 0.0), writes=("s_bf",), tag="s0b")

    def ln_apply(xin_ap, xout_ap, res_in, res_out, eps, tag, par=0):
        stat, x2s = stat2[par], x2s2[par]
        statr, x2sr = f"stat{par}", f"x2s{par}"
        snm, sv, sr = (stat[:, i:i + 1] for i in range(3))
        em.emit("dve", lambda h: h.reduce_sum(snm, xin_ap, axis=AX.X),
                reads=(res_in,), writes=(statr,), tag=f"{tag}a")
        em.emit("dve", lambda h: h.tensor_mul(x2s[:], xin_ap, xin_ap),
                reads=(res_in,), writes=(x2sr,), tag=f"{tag}b")
        em.emit("dve", lambda h: h.reduce_sum(sv, x2s[:], axis=AX.X),
                reads=(x2sr,), writes=(statr,), tag=f"{tag}c")
        em.emit("dve", lambda h: h.tensor_scalar_mul(snm, snm, -1.0 / D),
                reads=(statr,), writes=(statr,), tag=f"{tag}d")
        em.emit("dve", lambda h: h.tensor_scalar(sv, sv, 1.0 / D, float(eps),
                                                 ALU.mult, ALU.add),
                reads=(statr,), writes=(statr,), tag=f"{tag}e")
        em.emit("dve", lambda h: h.scalar_tensor_tensor(sr, snm, -1.0, snm,
                                                        ALU.mult, ALU.mult),
                reads=(statr,), writes=(statr,), tag=f"{tag}f")
        em.emit("dve", lambda h: h.tensor_sub(sv, sv, sr),
                reads=(statr,), writes=(statr,), tag=f"{tag}g")
        em.emit("act", lambda h: h.sqrt(sr, sv),
                reads=(statr,), writes=(statr,), tag=f"{tag}h")
        em.emit("dve", lambda h: h.reciprocal(sr, sr),
                reads=(statr,), writes=(statr,), tag=f"{tag}i")
        tnw = stat[:, 3:4]
        em.emit("dve", lambda h: h.tensor_mul(tnw, sr, sr),
                reads=(statr,), writes=(statr,), tag=f"{tag}n1")
        em.emit("dve", lambda h: h.tensor_mul(tnw, tnw, sv),
                reads=(statr,), writes=(statr,), tag=f"{tag}n2")
        em.emit("dve", lambda h: h.tensor_scalar(tnw, tnw, -0.5, 1.5,
                                                 ALU.mult, ALU.add),
                reads=(statr,), writes=(statr,), tag=f"{tag}n3")
        em.emit("dve", lambda h: h.tensor_mul(sr, sr, tnw),
                reads=(statr,), writes=(statr,), tag=f"{tag}n4")
        em.emit("dve", lambda h: h.tensor_scalar(xout_ap, xin_ap, snm, sr,
                                                 ALU.add, ALU.mult),
                reads=(res_in, statr), writes=(res_out,), tag=f"{tag}k")

    for G in range(NGROUP):
        xb, xres = x_tm[G % 2], f"x_tm{G % 2}"
        em.emit("sp", (lambda G=G, xb=xb: lambda h: h.dma_start(
            out=xb[:].rearrange("p (j d) -> p j d", d=D),
            in_=x_in[G * 512:(G + 1) * 512, :].rearrange(
                "(j p) d -> p j d", p=128)))(),
            writes=(xres,), dma=True, tag=f"ldx{G}")
        for j in range(4):
            xj = xb[:, j * D:(j + 1) * D]
            ln_apply(xj, xln_tm[:], xres, "xln_tm", EPS_LN, f"l1_{G}_{j}", par=j % 2)
            if debug:
                em.emit("act", (lambda G=G, j=j: lambda h: h.dma_start(
                    out=dbg["xln"][(G * 4 + j) * 128:(G * 4 + j + 1) * 128, :],
                    in_=xln_tm[:]))(),
                    reads=("xln_tm",), writes=("dbgout",), dma=True, tag=f"dxln{G}_{j}")
            for k in range(8):
                bk = k % 2
                em.emit("pe", (lambda k=k, bk=bk: lambda h: h.transpose(
                    psf[bk][:, 128:256], xln_tm[:, k * 128:(k + 1) * 128],
                    cst["identf"][:]))(),
                    reads=("xln_tm", "identf"), writes=(f"bank{bk}",),
                    tag=f"trx{G}_{j}_{k}")
                em.emit("dve", (lambda k=k, j=j, bk=bk: lambda h: h.tensor_copy(
                    fm_xln[k][:, j * 128:(j + 1) * 128], psf[bk][:, 128:256]))(),
                    reads=(f"bank{bk}",), writes=(f"fm_xln{k}",),
                    tag=f"trxe{G}_{j}_{k}")
        for m in range(16):
            bk = m % 2
            for k in range(8):
                em.emit("pe", (lambda m=m, k=k, bk=bk: lambda h: h.matmul(
                    psf[bk][:, :], W1b[k][:, m * 128:(m + 1) * 128], fm_xln[k][:],
                    start=(k == 0), stop=(k == 7)))(),
                    reads=(f"W1b{k}", f"fm_xln{k}"), writes=(f"bank{bk}",),
                    tag=f"a4_{G}_{m}_{k}")
            em.emit("dve", (lambda m=m, bk=bk: lambda h: h.tensor_scalar(
                fm_kq[m][:], psf[bk][:, :], cst["b1c"][:, m:m + 1], None, ALU.add))(),
                reads=(f"bank{bk}", "b1c"), writes=(f"fm_kq{m}",), tag=f"a4e_{G}_{m}")
            if debug:
                em.emit("act", (lambda G=G, m=m: lambda h: h.dma_start(
                    out=dbg["kq"][G, m], in_=fm_kq[m][:]))(),
                    reads=(f"fm_kq{m}",), writes=("dbgout",), dma=True, tag=f"dkq{G}_{m}")
        for j in range(4):
            g = G * 4 + j
            for piece in range(4):
                col0 = (0 if piece < 2 else 2 * D) + (piece % 2) * 512
                for k in range(8):
                    em.emit("pe", (lambda k=k, piece=piece, col0=col0, j=j:
                            lambda h: h.matmul(
                                psf[piece][:, :], fm_xln[k][:, j * 128:(j + 1) * 128],
                                W1b[k][:, col0:col0 + 512],
                                start=(k == 0), stop=(k == 7)))(),
                            reads=(f"fm_xln{k}", f"W1b{k}"), writes=(f"bank{piece}",),
                            tag=f"a5_{g}_{piece}_{k}")
            ktm, vtm = ktm2[j % 2], vtm2[j % 2]
            kres, vres = f"ktm{j % 2}", f"vtm{j % 2}"
            for piece in range(4):
                dst, dres = (ktm, kres) if piece < 2 else (vtm, vres)
                boff = (0 if piece < 2 else D) + (piece % 2) * 512
                em.emit("dve", (lambda dst=dst, piece=piece, boff=boff:
                        lambda h: h.tensor_add(
                            dst[:, (piece % 2) * 512:(piece % 2) * 512 + 512],
                            psf[piece][:, :], cst["b1kv"][:, boff:boff + 512]))(),
                        reads=(f"bank{piece}", "b1kv"), writes=(dres,),
                        tag=f"a5e_{g}_{piece}")
            for h_ in range(NH):
                bh = (h_ % 2) * 64
                mq, mk_ = 8 + h_ // 2, h_ // 2
                bnk = 4 if h_ % 2 == 0 else 2
                a_sb, ares = a_sb2[h_ % 2], f"a_sb{h_ % 2}"
                em.emit("pe", (lambda h_=h_, j=j, bh=bh, mq=mq, mk_=mk_, bnk=bnk:
                        lambda h: h.matmul(
                            psf[bnk][:, 0:128],
                            fm_kq[mk_][bh:bh + 64, j * 128:(j + 1) * 128],
                            fm_kq[mq][bh:bh + 64, j * 128:(j + 1) * 128],
                            start=True, stop=True))(),
                        reads=(f"fm_kq{mk_}", f"fm_kq{mq}"), writes=(f"bank{bnk}",),
                        tag=f"at{g}_{h_}")
                em.emit("dve", (lambda bnk=bnk, a_sb=a_sb: lambda h: h.tensor_mul(
                    a_sb[:], psf[bnk][:, 0:128], cst["utmask"][:]))(),
                    reads=(f"bank{bnk}", "utmask"), writes=(ares,),
                    tag=f"atm{g}_{h_}")
                em.emit("pe", (lambda h_=h_, bh=bh, bnk=bnk, g=g, vtm=vtm,
                        a_sb=a_sb: lambda h: h.matmul(
                    psf[bnk][bh:bh + 64, 128:256], vtm[:, h_ * 64:(h_ + 1) * 64],
                    a_sb[:], start=True, stop=(g == 0)))(),
                    reads=(vres, ares), writes=(f"bank{bnk}",), tag=f"oi{g}_{h_}")
                if g > 0:
                    em.emit("pe", (lambda h_=h_, j=j, bh=bh, mq=mq, bnk=bnk:
                            lambda h: h.matmul(
                                psf[bnk][bh:bh + 64, 128:256],
                                s_bf[bh:bh + 64, (h_ // 2) * 64:(h_ // 2) * 64 + 64],
                                fm_kq[mq][bh:bh + 64, j * 128:(j + 1) * 128],
                                start=False, stop=True))(),
                            reads=("s_bf", f"fm_kq{mq}"), writes=(f"bank{bnk}",),
                            tag=f"os{g}_{h_}")
                em.emit("dve", (lambda h_=h_, j=j, bh=bh, bnk=bnk:
                        lambda h: h.tensor_copy(
                            fm_am[h_ // 2][bh:bh + 64, j * 128:(j + 1) * 128],
                            psf[bnk][bh:bh + 64, 128:256]))(),
                        reads=(f"bank{bnk}",), writes=(f"fm_am{h_ // 2}",),
                        tag=f"oe{g}_{h_}")
                em.emit("pe", (lambda h_=h_, bh=bh, ktm=ktm, vtm=vtm:
                        lambda h: h.matmul(
                    psf[5][bh:bh + 64, (h_ // 2) * 64:(h_ // 2) * 64 + 64],
                    ktm[:, h_ * 64:(h_ + 1) * 64], vtm[:, h_ * 64:(h_ + 1) * 64],
                    start=True, stop=True))(),
                    reads=(kres, vres), writes=("bank5",), tag=f"su{g}_{h_}")
            if g < NCHUNK - 1:
                em.emit("dve", lambda h: h.tensor_add(s_f32[:], s_f32[:], psf[5][:, :]),
                        reads=("bank5", "s_f32"), writes=("s_f32",), tag=f"sa{g}")
                em.emit("dve", lambda h: h.tensor_copy(s_bf[:], s_f32[:]),
                        reads=("s_f32",), writes=("s_bf",), tag=f"sb{g}")
            for nh in range(2):
                for k in range(8):
                    em.emit("pe", (lambda k=k, nh=nh, j=j: lambda h: h.matmul(
                        psf[nh][:, :], fm_am[k][:, j * 128:(j + 1) * 128],
                        W2b[k][:, nh * 512:(nh + 1) * 512],
                        start=(k == 0), stop=(k == 7)))(),
                        reads=(f"fm_am{k}", f"W2b{k}"), writes=(f"bank{nh}",),
                        tag=f"w2_{g}_{nh}_{k}")
                em.emit("dve", (lambda nh=nh: lambda h: h.tensor_add(
                    x1_tm[:, nh * 512:(nh + 1) * 512], psf[nh][:, :],
                    cst["b2bc"][:, nh * 512:(nh + 1) * 512]))(),
                    reads=(f"bank{nh}", "b2bc"), writes=("x1_tm",), tag=f"w2e_{g}_{nh}")
            if debug:
                em.emit("act", (lambda g=g: lambda h: h.dma_start(
                    out=dbg["ktm"][g * 128:(g + 1) * 128, :], in_=ktm[:]))(),
                    reads=("ktm",), writes=("dbgout",), dma=True, tag=f"dktm{g}")
                em.emit("act", (lambda g=g: lambda h: h.dma_start(
                    out=dbg["vtm"][g * 128:(g + 1) * 128, :], in_=vtm[:]))(),
                    reads=("vtm",), writes=("dbgout",), dma=True, tag=f"dvtm{g}")
                if g % 4 == 3:
                    for k in range(8):
                        em.emit("act", (lambda G=G, k=k: lambda h: h.dma_start(
                            out=dbg["am"][G, k], in_=fm_am[k][:]))(),
                            reads=(f"fm_am{k}",), writes=("dbgout",), dma=True, tag=f"dam{G}_{k}")
            em.emit("dve", (lambda j=j, xb=xb, xres=xres: lambda h: h.tensor_add(
                x1_tm[:], x1_tm[:], xb[:, j * D:(j + 1) * D]))(),
                reads=("x1_tm", xres), writes=("x1_tm",), tag=f"res{g}")
            em.emit("sp", (lambda g=g: lambda h: h.dma_start(
                out=x1d[g * 128:(g + 1) * 128, :], in_=x1_tm[:]))(),
                reads=("x1_tm",), writes=("x1d",), dma=True, tag=f"sx1_{g}")
            ln_apply(x1_tm[:], xf32[:], "x1_tm", "xf32", EPS_LN, f"l2_{g}", par=g % 2)
            em.emit("dve", lambda h: h.tensor_copy(xf_tm[:], xf32[:]),
                    reads=("xf32",), writes=("xf_tm",), tag=f"xfc{g}")
            em.emit("act", (lambda g=g: lambda h: h.dma_start(
                out=xfd[g * 128:(g + 1) * 128, :], in_=xf_tm[:]))(),
                reads=("xf_tm",), writes=("xfd",), dma=True, tag=f"sxf_{g}")
            x2sb, x2sbr = x2s2[g % 2], f"x2s{g % 2}"
            em.emit("dve", (lambda x2sb=x2sb: lambda h: h.tensor_mul(
                x2sb[:], xf_tm[:], xf_tm[:]))(),
                reads=("xf_tm",), writes=(x2sbr,), tag=f"rq{g}")
            em.emit("dve", (lambda x2sb=x2sb: lambda h: h.reduce_sum(
                srow[:, 0:1], x2sb[:], axis=AX.X))(),
                reads=(x2sbr,), writes=("srow",), tag=f"rr{g}")
            em.emit("dve", lambda h: h.tensor_scalar(srow[:, 0:1], srow[:, 0:1],
                                                     1.0 / D, EPS_RMS, ALU.mult, ALU.add),
                    reads=("srow",), writes=("srow",), tag=f"rv{g}")
            em.emit("act", lambda h: h.sqrt(srow[:, 1:2], srow[:, 0:1]),
                    reads=("srow",), writes=("srow",), tag=f"rs{g}")
            em.emit("dve", lambda h: h.reciprocal(srow[:, 1:2], srow[:, 1:2]),
                    reads=("srow",), writes=("srow",), tag=f"rc{g}")
            em.emit("dve", lambda h: h.tensor_mul(srow[:, 2:3], srow[:, 1:2],
                                                  srow[:, 1:2]),
                    reads=("srow",), writes=("srow",), tag=f"rn1{g}")
            em.emit("dve", lambda h: h.tensor_mul(srow[:, 2:3], srow[:, 2:3],
                                                  srow[:, 0:1]),
                    reads=("srow",), writes=("srow",), tag=f"rn2{g}")
            em.emit("dve", lambda h: h.tensor_scalar(srow[:, 2:3], srow[:, 2:3],
                                                     -0.5, 1.5, ALU.mult, ALU.add),
                    reads=("srow",), writes=("srow",), tag=f"rn3{g}")
            em.emit("dve", lambda h: h.tensor_mul(srow[:, 1:2], srow[:, 1:2],
                                                  srow[:, 2:3]),
                    reads=("srow",), writes=("srow",), tag=f"rn4{g}")
            for k in range(8):
                tb = k % 2
                em.emit("pe", (lambda k=k, tb=tb: lambda h: h.transpose(
                    psf[tb][:, 128:256], xf32[:, k * 128:(k + 1) * 128],
                    cst["identf"][:]))(),
                    reads=("xf32", "identf"), writes=(f"bank{tb}",), tag=f"tf{g}_{k}")
                em.emit("dve", (lambda k=k, tb=tb: lambda h: h.tensor_copy(
                    fm_xfc[k][:], psf[tb][:, 128:256]))(),
                    reads=(f"bank{tb}",), writes=(f"fm_xfc{k}",), tag=f"tfe{g}_{k}")
            for k in range(8):
                em.emit("pe", (lambda k=k: lambda h: h.matmul(
                    psf[3][:, 0:8], fm_xfc[k][:], rwb[:, k * 8:(k + 1) * 8],
                    start=(k == 0), stop=(k == 7)))(),
                    reads=(f"fm_xfc{k}", "rwb"), writes=("bank3",), tag=f"lg{g}_{k}")
            em.emit("dve", (lambda g=g: lambda h: h.tensor_scalar_mul(
                lg_sb[:, g * 8:(g + 1) * 8], psf[3][:, 0:8], srow[:, 1:2]))(),
                reads=("bank3", "srow"), writes=("lg_sb",), tag=f"lge{g}")

    m1L, m2L = sc2[:, 0:32], sc3[:, 0:32]
    em.emit("dve", lambda h: h.tensor_copy(m1L, lg_sb[:, 0::8]),
            reads=("lg_sb",), writes=("sc2",), tag="Lm0")
    for e_ in range(1, 8):
        em.emit("dve", (lambda e_=e_: lambda h: h.tensor_max(
            m1L, m1L, lg_sb[:, e_::8]))(),
            reads=("lg_sb", "sc2"), writes=("sc2",), tag=f"Lm{e_}")
    for e_ in range(8):
        em.emit("dve", (lambda e_=e_: lambda h: h.tensor_tensor(
            eq1L[:, e_::8], lg_sb[:, e_::8], m1L, ALU.is_equal))(),
            reads=("lg_sb", "sc2"), writes=("eq1L",), tag=f"Le1{e_}")
    for e_ in range(8):
        em.emit("dve", (lambda e_=e_: lambda h: h.scalar_tensor_tensor(
            sc1[:, e_::8], eq1L[:, e_::8], -BIG, lg_sb[:, e_::8],
            ALU.mult, ALU.add))(),
            reads=("eq1L", "lg_sb"), writes=("sc1",), tag=f"Lk{e_}")
    em.emit("dve", lambda h: h.tensor_copy(m2L, sc1[:, 0::8]),
            reads=("sc1",), writes=("sc3",), tag="Ln0")
    for e_ in range(1, 8):
        em.emit("dve", (lambda e_=e_: lambda h: h.tensor_max(
            m2L, m2L, sc1[:, e_::8]))(),
            reads=("sc1", "sc3"), writes=("sc3",), tag=f"Ln{e_}")
    for e_ in range(8):
        em.emit("dve", (lambda e_=e_: lambda h: h.tensor_tensor(
            eq2L[:, e_::8], lg_sb[:, e_::8], m2L, ALU.is_equal))(),
            reads=("lg_sb", "sc3"), writes=("eq2L",), tag=f"Le2{e_}")
    em.emit("dve", lambda h: h.tensor_add(sc1[:], eq1L[:], eq2L[:]),
            reads=("eq1L", "eq2L"), writes=("sc1",), tag="Lsel")
    em.emit("pe", lambda h: h.matmul(psf[0][:, 0:256], cst["ltstrict"][:], sc1[:],
                                     start=True, stop=True),
            reads=("ltstrict", "sc1"), writes=("bank0",), tag="Lcs")
    em.emit("pe", lambda h: h.matmul(psf[1][0:1, 0:256], cst["ones_col"][:], sc1[:],
                                     start=True, stop=True),
            reads=("ones_col", "sc1"), writes=("bank1",), tag="Ltot")
    em.emit("dve", lambda h: h.tensor_copy(sc2[0:1, 0:256], psf[1][0:1, 0:256]),
            reads=("bank1",), writes=("sc2",), tag="Ltote")
    em.emit("dve", lambda h: h.tensor_copy(srow[0:1, 8:16], sc2[0:1, 0:8]),
            reads=("sc2",), writes=("srow",), tag="Lc0")
    for jj in range(1, 32):
        em.emit("dve", (lambda jj=jj: lambda h: h.tensor_add(
            srow[0:1, 8:16], srow[0:1, 8:16], sc2[0:1, jj * 8:(jj + 1) * 8]))(),
            reads=("srow", "sc2"), writes=("srow",), tag=f"Lc{jj}")
    for piece in range(2):
        em.emit("pe", (lambda piece=piece: lambda h: h.transpose(
            psf[2][:, 256 + piece:257 + piece],
            sc2[0:1, piece * 128:(piece + 1) * 128], cst["identf"][0:1, 0:1]))(),
            reads=("sc2", "identf"), writes=("bank2",), tag=f"Ltt{piece}")
    em.emit("dve", lambda h: h.tensor_copy(sc3[:, 0:2], psf[2][:, 256:258]),
            reads=("bank2",), writes=("sc3",), tag="Ltte")
    em.emit("pe", lambda h: h.matmul(psf[2][:, 0:2], cst["lt8"][:], sc3[:, 0:2],
                                     start=True, stop=True),
            reads=("lt8", "sc3"), writes=("bank2",), tag="Lo1")
    em.emit("pe", lambda h: h.matmul(psf[2][:, 2:3], cst["cs8"][:], sc3[:, 0:1],
                                     start=True, stop=True),
            reads=("cs8", "sc3"), writes=("bank2",), tag="Lo2")
    em.emit("dve", lambda h: h.tensor_copy(sc4[:, 0:2], psf[2][:, 0:2]),
            reads=("bank2",), writes=("sc4",), tag="Loe")
    em.emit("dve", lambda h: h.tensor_add(sc4[:, 1:2], sc4[:, 1:2], psf[2][:, 2:3]),
            reads=("bank2", "sc4"), writes=("sc4",), tag="Loa")
    for piece in range(2):
        em.emit("pe", (lambda piece=piece: lambda h: h.transpose(
            psf[2][0:1, 128 + piece * 128:256 + piece * 128],
            sc4[:, piece:piece + 1], cst["identf"][:]))(),
            reads=("sc4", "identf"), writes=("bank2",), tag=f"Lb{piece}")
    em.emit("dve", lambda h: h.tensor_copy(sc2[0:1, 0:256], psf[2][0:1, 128:384]),
            reads=("bank2",), writes=("sc2",), tag="Lbe")
    em.emit("pe", lambda h: h.matmul(psf[3][:, 0:256], cst["ones_row"][:],
                                     sc2[0:1, 0:256], start=True, stop=True),
            reads=("ones_row", "sc2"), writes=("bank3",), tag="Lbc")
    em.emit("dve", lambda h: h.tensor_copy(rnkL[:], psf[0][:, 0:256]),
            reads=("bank0",), writes=("rnkL",), tag="Lr0")
    em.emit("dve", lambda h: h.tensor_add(rnkL[:], rnkL[:], psf[3][:, 0:256]),
            reads=("bank3", "rnkL"), writes=("rnkL",), tag="Lr1")
    em.emit("sp", lambda h: h.dma_start(out=lgd[0:128, :], in_=lg_sb[:]),
            reads=("lg_sb",), writes=("lgd",), dma=True, tag="slg")
    em.emit("sp", lambda h: h.dma_start(out=lgd[128:129, 0:8], in_=srow[0:1, 8:16]),
            reads=("srow",), writes=("lgd",), dma=True, tag="scnt")

    em.emit("pool", lambda h: h.collective_compute(
        "AllGather", ALU.bypass, replica_groups=[list(range(8))],
        ins=[lgd.opt()], outs=[lg_all.opt()]),
        reads=("lgd",), writes=("lg_all",), cc=True, tag="AG1a")
    em.emit("pool", lambda h: h.collective_compute(
        "AllGather", ALU.bypass, replica_groups=[list(range(8))],
        ins=[xfd.opt()], outs=[xf_all.opt()]),
        reads=("xfd",), writes=("xf_all",), cc=True, tag="AG1b")

    if debug:
        em.emit("act", lambda h: h.dma_start(out=dbg["x1"], in_=x1d),
                reads=("x1d",), writes=("dbgout",), dma=True, tag="dbgx1")
        em.emit("act", lambda h: h.dma_start(out=dbg["xf"], in_=xfd),
                reads=("xfd",), writes=("dbgout",), dma=True, tag="dbgxf")
        em.emit("act", lambda h: h.dma_start(out=dbg["lg"], in_=lgd),
                reads=("lgd",), writes=("dbgout",), dma=True, tag="dbglg")

    sb_free(*(f"W1b{k}" for k in range(8)), *(f"W2b{k}" for k in range(8)),
            "x_tm0", "x_tm1", "xln_tm", *(f"fm_xln{k}" for k in range(8)),
            *(f"fm_kq{m}" for m in range(16)), "ktm0", "ktm1", "vtm0", "vtm1",
            "a_sb0", "a_sb1", "s_f32", "s_bf", *(f"fm_am{k}" for k in range(8)),
            "x1_tm", "xf_tm", "xf32", *(f"fm_xfc{k}" for k in range(8)),
            "stat0", "stat1", "x2s0", "x2s1")
    em.barrier()

    lgt = sb("lgt", [128, 2048], F32)
    gsc = sb("gsc", [128, 2048], F32)
    geq1 = sb("geq1", [128, 2048], F32)
    geq2 = sb("geq2", [128, 2048], F32)
    selc = sb("selc", [128, 256], F32)
    wdc = sb("wdc", [128, 256], F32)
    rnkc = sb("rnkc", [128, 256], F32)
    gm1 = sb("gm1", [128, 256], F32)
    gm2 = sb("gm2", [128, 256], F32)
    cnt8 = sb("cnt8", [8, 8], F32)
    base8 = sb("base8", [128, 8], F32)
    agb = [sb(f"ag{i}", [128, 128], F32) for i in range(4)]
    bgb = [sb(f"bg{i}", [128, 2 * NTC], F32) for i in range(4)]

    em.emit("sp", lambda h: h.dma_start(
        out=lgt[:].rearrange("q (r n) -> q r n", n=256),
        in_=lg_all.rearrange("(r q) n -> q r n", q=129)[0:128, :, :]),
        reads=("lg_all",), writes=("lgt",), dma=True, tag="ldlgt")
    em.emit("sp", lambda h: h.dma_start(
        out=cnt8[:, :], in_=lg_all.rearrange("(r q) n -> r q n", q=129)[:, 128, 0:8]),
        reads=("lg_all",), writes=("cnt8",), dma=True, tag="ldcnt")

    em.emit("dve", lambda h: h.tensor_copy(gm1[:], lgt[:, 0::8]),
            reads=("lgt",), writes=("gm1",), tag="Gm0")
    for e_ in range(1, 8):
        em.emit("dve", (lambda e_=e_: lambda h: h.tensor_max(
            gm1[:], gm1[:], lgt[:, e_::8]))(),
            reads=("lgt", "gm1"), writes=("gm1",), tag=f"Gm{e_}")
    for e_ in range(8):
        em.emit("dve", (lambda e_=e_: lambda h: h.tensor_tensor(
            geq1[:, e_::8], lgt[:, e_::8], gm1[:], ALU.is_equal))(),
            reads=("lgt", "gm1"), writes=("geq1",), tag=f"Ge1{e_}")
    for e_ in range(8):
        em.emit("dve", (lambda e_=e_: lambda h: h.scalar_tensor_tensor(
            gsc[:, e_::8], geq1[:, e_::8], -BIG, lgt[:, e_::8], ALU.mult, ALU.add))(),
            reads=("geq1", "lgt"), writes=("gsc",), tag=f"Gk{e_}")
    em.emit("dve", lambda h: h.tensor_copy(gm2[:], gsc[:, 0::8]),
            reads=("gsc",), writes=("gm2",), tag="Gn0")
    for e_ in range(1, 8):
        em.emit("dve", (lambda e_=e_: lambda h: h.tensor_max(
            gm2[:], gm2[:], gsc[:, e_::8]))(),
            reads=("gsc", "gm2"), writes=("gm2",), tag=f"Gn{e_}")
    for e_ in range(8):
        em.emit("dve", (lambda e_=e_: lambda h: h.tensor_tensor(
            geq2[:, e_::8], lgt[:, e_::8], gm2[:], ALU.is_equal))(),
            reads=("lgt", "gm2"), writes=("geq2",), tag=f"Ge2{e_}")
    em.emit("dve", lambda h: h.tensor_sub(sc1[:], gm1[:], gm2[:]),
            reads=("gm1", "gm2"), writes=("sc1",), tag="Gd")
    em.emit("act", lambda h: h.activation(sc1[:], sc1[:], AF.Sigmoid),
            reads=("sc1",), writes=("sc1",), tag="Gsig")
    em.emit("dve", lambda h: h.tensor_scalar(sc2[:], sc1[:], -1.0, 1.0,
                                             ALU.mult, ALU.add),
            reads=("sc1",), writes=("sc2",), tag="Grw2")
    em.emit("dve", lambda h: h.memset(selc[:], 0.0), writes=("selc",), tag="Gs0")
    em.emit("dve", lambda h: h.memset(wdc[:], 0.0), writes=("wdc",), tag="Gw0")
    for e_ in range(8):
        em.emit("dve", (lambda e_=e_: lambda h: h.scalar_tensor_tensor(
            sc3[:], geq1[:, e_::8], 1.0, geq2[:, e_::8], ALU.mult, ALU.add))(),
            reads=("geq1", "geq2"), writes=("sc3",), tag=f"Gse{e_}")
        em.emit("dve", (lambda e_=e_: lambda h: h.scalar_tensor_tensor(
            selc[:], sc3[:], cst["csel"][:, e_:e_ + 1], selc[:], ALU.mult, ALU.add))(),
            reads=("sc3", "csel", "selc"), writes=("selc",), tag=f"Gsa{e_}")
        em.emit("dve", (lambda e_=e_: lambda h: h.tensor_mul(
            sc3[:], geq1[:, e_::8], sc1[:]))(),
            reads=("geq1", "sc1"), writes=("sc3",), tag=f"Gw1{e_}")
        em.emit("dve", (lambda e_=e_: lambda h: h.tensor_mul(
            sc4[:], geq2[:, e_::8], sc2[:]))(),
            reads=("geq2", "sc2"), writes=("sc4",), tag=f"Gw3{e_}")
        em.emit("dve", lambda h: h.tensor_add(sc3[:], sc3[:], sc4[:]),
                reads=("sc3", "sc4"), writes=("sc3",), tag=f"Gw4{e_}")
        em.emit("dve", (lambda e_=e_: lambda h: h.scalar_tensor_tensor(
            wdc[:], sc3[:], cst["csel"][:, e_:e_ + 1], wdc[:], ALU.mult, ALU.add))(),
            reads=("sc3", "csel", "wdc"), writes=("wdc",), tag=f"Gw5{e_}")
    em.emit("pe", lambda h: h.matmul(psf[0][:, 0:256], cst["ltstrict"][:], selc[:],
                                     start=True, stop=True),
            reads=("ltstrict", "selc"), writes=("bank0",), tag="Gcs")
    em.emit("pe", lambda h: h.matmul(psf[1][0:1, 0:256], cst["ones_col"][:], selc[:],
                                     start=True, stop=True),
            reads=("ones_col", "selc"), writes=("bank1",), tag="Gtot")
    em.emit("dve", lambda h: h.tensor_copy(sc2[0:1, 0:256], psf[1][0:1, 0:256]),
            reads=("bank1",), writes=("sc2",), tag="Gtote")
    for piece in range(2):
        em.emit("pe", (lambda piece=piece: lambda h: h.transpose(
            psf[2][:, 256 + piece:257 + piece],
            sc2[0:1, piece * 128:(piece + 1) * 128], cst["identf"][0:1, 0:1]))(),
            reads=("sc2", "identf"), writes=("bank2",), tag=f"Gtt{piece}")
    em.emit("dve", lambda h: h.tensor_copy(sc3[:, 0:2], psf[2][:, 256:258]),
            reads=("bank2",), writes=("sc3",), tag="Gtte")
    em.emit("pe", lambda h: h.matmul(psf[2][:, 0:2], cst["ltstrict"][:], sc3[:, 0:2],
                                     start=True, stop=True),
            reads=("ltstrict", "sc3"), writes=("bank2",), tag="Go1")
    em.emit("pe", lambda h: h.matmul(psf[4][0:1, 0:1], cst["ones_col"][:], sc3[:, 0:1],
                                     start=True, stop=True),
            reads=("ones_col", "sc3"), writes=("bank4",), tag="Go2")
    em.emit("dve", lambda h: h.tensor_copy(srow[0:1, 2:3], psf[4][0:1, 0:1]),
            reads=("bank4",), writes=("srow",), tag="Go2e")
    em.emit("pe", lambda h: h.matmul(psf[4][:, 0:1], cst["ones_row"][:],
                                     srow[0:1, 2:3], start=True, stop=True),
            reads=("ones_row", "srow"), writes=("bank4",), tag="Go2b")
    em.emit("dve", lambda h: h.tensor_copy(sc4[:, 0:2], psf[2][:, 0:2]),
            reads=("bank2",), writes=("sc4",), tag="Goe")
    em.emit("dve", lambda h: h.tensor_add(sc4[:, 1:2], sc4[:, 1:2], psf[4][:, 0:1]),
            reads=("bank4", "sc4"), writes=("sc4",), tag="Goa")
    for piece in range(2):
        em.emit("pe", (lambda piece=piece: lambda h: h.transpose(
            psf[2][0:1, 128 + piece * 128:256 + piece * 128],
            sc4[:, piece:piece + 1], cst["identf"][:]))(),
            reads=("sc4", "identf"), writes=("bank2",), tag=f"Gb{piece}")
    em.emit("dve", lambda h: h.tensor_copy(sc2[0:1, 0:256], psf[2][0:1, 128:384]),
            reads=("bank2",), writes=("sc2",), tag="Gbe")
    em.emit("pe", lambda h: h.matmul(psf[3][:, 0:256], cst["ones_row"][:],
                                     sc2[0:1, 0:256], start=True, stop=True),
            reads=("ones_row", "sc2"), writes=("bank3",), tag="Gbc")
    em.emit("dve", lambda h: h.tensor_copy(rnkc[:], psf[0][:, 0:256]),
            reads=("bank0",), writes=("rnkc",), tag="Gr0")
    em.emit("dve", lambda h: h.tensor_add(rnkc[:], rnkc[:], psf[3][:, 0:256]),
            reads=("bank3", "rnkc"), writes=("rnkc",), tag="Gr1")
    em.emit("pe", lambda h: h.matmul(psf[4][0:1, 0:8], cst["prevmask"][:], cnt8[:, :],
                                     start=True, stop=True),
            reads=("prevmask", "cnt8"), writes=("bank4",), tag="Gba")
    em.emit("dve", lambda h: h.tensor_copy(srow[0:1, 8:16], psf[4][0:1, 0:8]),
            reads=("bank4",), writes=("srow",), tag="Gbb")
    em.emit("pe", lambda h: h.matmul(psf[4][:, 0:8], cst["ones_row"][:],
                                     srow[0:1, 8:16], start=True, stop=True),
            reads=("ones_row", "srow"), writes=("bank4",), tag="Gbc2")
    em.emit("dve", lambda h: h.tensor_copy(base8[:], psf[4][:, 0:8]),
            reads=("bank4",), writes=("base8",), tag="Gbd")
    for jj in range(NCHUNK):
        em.emit("dve", (lambda jj=jj: lambda h: h.tensor_add(
            rnkL[:, jj * 8:(jj + 1) * 8], rnkL[:, jj * 8:(jj + 1) * 8], base8[:]))(),
            reads=("rnkL", "base8"), writes=("rnkL",), tag=f"Grl{jj}")

    em.emit("dve", lambda h: h.tensor_single_scalar(sc1[:], rnkc[:], float(CAP),
                                                    ALU.is_lt),
            reads=("rnkc",), writes=("sc1",), tag="Dk")
    em.emit("dve", lambda h: h.tensor_mul(sc1[:], sc1[:], selc[:]),
            reads=("sc1", "selc"), writes=("sc1",), tag="Dk2")
    em.emit("dve", lambda h: h.tensor_mul(sc2[:], rnkc[:], sc1[:]),
            reads=("rnkc", "sc1"), writes=("sc2",), tag="Dm1")
    em.emit("dve", lambda h: h.tensor_scalar(sc3[:], sc1[:], -BIG, BIG,
                                             ALU.mult, ALU.add),
            reads=("sc1",), writes=("sc3",), tag="Dm1b")
    em.emit("dve", lambda h: h.tensor_add(sc2[:], sc2[:], sc3[:]),
            reads=("sc2", "sc3"), writes=("sc2",), tag="Dm2")
    idiv = sb("idiv", [128, 256], I32)
    em.emit("dve", lambda h: h.tensor_scalar(sc3[:], sc2[:], 1.0 / 128.0, -0.4999,
                                             ALU.mult, ALU.add),
            reads=("sc2",), writes=("sc3",), tag="Dd1")
    em.emit("dve", lambda h: h.tensor_copy(idiv[:], sc3[:]),
            reads=("sc3",), writes=("idiv",), tag="Dd2")
    em.emit("dve", lambda h: h.tensor_copy(sc3[:], idiv[:]),
            reads=("idiv",), writes=("sc3",), tag="Dd3")
    em.emit("dve", lambda h: h.scalar_tensor_tensor(sc4[:], sc3[:], -128.0, sc2[:],
                                                    ALU.mult, ALU.add),
            reads=("sc3", "sc2"), writes=("sc4",), tag="Dd4")
    for g in range(256):
        ag, bg = agb[g % 4], bgb[g % 4]
        ar, br = f"agg{g % 4}", f"bgg{g % 4}"
        em.emit("dve", (lambda g=g, ag=ag: lambda h: h.tensor_scalar(
            ag[:], cst["iota_f128"][:], sc4[:, g:g + 1], None, ALU.is_equal))(),
            reads=("sc4", "iota_f128"), writes=(ar,), tag=f"Da{g}")
        em.emit("dve", (lambda g=g, bg=bg: lambda h: h.tensor_scalar(
            bg[:, 0:NTC], cst["iota_f40"][:], sc3[:, g:g + 1], None, ALU.is_equal))(),
            reads=("sc3", "iota_f40"), writes=(br,), tag=f"Db{g}")
        em.emit("dve", (lambda g=g, bg=bg: lambda h: h.tensor_scalar_mul(
            bg[:, NTC:2 * NTC], bg[:, 0:NTC], cst["tok_pg"][:, g:g + 1]))(),
            reads=(br, "tok_pg"), writes=(br,), tag=f"Dbt{g}")
        em.emit("dve", (lambda g=g, bg=bg: lambda h: h.tensor_scalar_mul(
            bg[:, 0:NTC], bg[:, 0:NTC], wdc[:, g:g + 1]))(),
            reads=(br, "wdc"), writes=(br,), tag=f"Dbw{g}")
        em.emit("pe", (lambda g=g, ag=ag, bg=bg: lambda h: h.matmul(
            psf[5][:, 0:NTC], ag[:], bg[:, NTC:2 * NTC],
            start=(g == 0), stop=(g == 255)))(),
            reads=(ar, br), writes=("bank5",), tag=f"Dp{g}")
        em.emit("pe", (lambda g=g, ag=ag, bg=bg: lambda h: h.matmul(
            psf[6][:, 0:NTC], ag[:], bg[:, 0:NTC],
            start=(g == 0), stop=(g == 255)))(),
            reads=(ar, br), writes=("bank6",), tag=f"Dq{g}")
    em.emit("dve", lambda h: h.tensor_copy(disp_i[:], psf[5][:, 0:NTC]),
            reads=("bank5",), writes=("disp_i",), tag="Dde")
    em.emit("dve", lambda h: h.tensor_copy(wv_sb[:], psf[6][:, 0:NTC]),
            reads=("bank6",), writes=("wv_sb",), tag="Dwe")

    if debug:
        em.emit("act", lambda h: h.dma_start(out=dbg["disp"], in_=disp_i[:]),
                reads=("disp_i",), writes=("dbgout",), dma=True, tag="dbgdisp")
        em.emit("act", lambda h: h.dma_start(out=dbg["wv"], in_=wv_sb[:]),
                reads=("wv_sb",), writes=("dbgout",), dma=True, tag="dbgwv")
        em.emit("act", lambda h: h.dma_start(out=dbg["rnk"], in_=rnkc[:]),
                reads=("rnkc",), writes=("dbgout",), dma=True, tag="dbgrnk")
        em.emit("act", lambda h: h.dma_start(out=dbg["sel"], in_=selc[:]),
                reads=("selc",), writes=("dbgout",), dma=True, tag="dbgsel")
        em.emit("act", lambda h: h.dma_start(out=dbg["wdc"], in_=wdc[:]),
                reads=("wdc",), writes=("dbgout",), dma=True, tag="dbgwdc")

    sb_free("lgt", "gsc", "geq1", "geq2", "gm1", "gm2", "cnt8", "base8",
            "ag0", "ag1", "ag2", "ag3", "bg0", "bg1", "bg2", "bg3",
            "idiv", "selc", "wdc", "rnkc")
    em.barrier()

    NGB = 10
    xg_tm = [sb(f"xg_tm{i}", [128, D], BF16) for i in range(2)]
    fm_xg = [sb(f"fm_xg{k}", [128, 512], BF16) for k in range(8)]
    guB = sb("guB", [128, 32 * 512], BF16)
    rgw = [sb(f"rgw{i}", [128, 1024], BF16) for i in range(3)]
    ruw = [sb(f"ruw{i}", [128, 1024], BF16) for i in range(3)]
    rdw = [sb(f"rdw{i}", [128, 1024], BF16) for i in range(3)]
    gsil = [sb(f"gsil{i}", [128, 512], BF16) for i in range(2)]
    ct_bf = sb("ct_bf", [128, D], BF16)

    for grp in range(NGB):
        for t4 in range(4):
            tile = grp * 4 + t4
            xr = f"xg{t4 % 2}"
            em.emit("pool", (lambda tile=tile, t4=t4: lambda h: h.indirect_dma_start(
                out=xg_tm[t4 % 2][:], out_offset=None, in_=xf_all,
                in_offset=bass.IndirectOffsetOnAxis(disp_i[:, tile:tile + 1], axis=0),
                bounds_check=regs["xg"], oob_is_err=False))(),
                reads=("xf_all", "disp_i"), writes=(xr,), dma=True, tag=f"xg{tile}")
            for k in range(8):
                em.emit("pe", (lambda k=k, t4=t4: lambda h: h.transpose(
                    psb[:, 0:128], xg_tm[t4 % 2][:, k * 128:(k + 1) * 128],
                    cst["identb"][:]))(),
                    reads=(xr, "identb"), writes=(BKB,), tag=f"xgt{tile}_{k}")
                em.emit("dve", (lambda k=k, t4=t4: lambda h: h.tensor_copy(
                    fm_xg[k][:, t4 * 128:(t4 + 1) * 128], psb[:, 0:128]))(),
                    reads=(BKB,), writes=(f"fm_xg{k}",), tag=f"xgte{tile}_{k}")
        for i in range(32):
            wb, wres = rgw[i % 3], f"rgw{i % 3}"
            ub, ures = ruw[i % 3], f"ruw{i % 3}"
            em.emit("sp", (lambda i=i, wb=wb: lambda h: h.dma_start(
                out=wb[:], in_=rgb[i].rearrange("p dk f -> p (dk f)")))(),
                reads=("rgb",), writes=(wres,), dma=True, tag=f"lrg{grp}_{i}")
            em.emit("act", (lambda i=i, ub=ub: lambda h: h.dma_start(
                out=ub[:], in_=rub[i].rearrange("p dk f -> p (dk f)")))(),
                reads=("rub",), writes=(ures,), dma=True, tag=f"lru{grp}_{i}")
            bg_, bu_ = ((0, 1), (2, 3), (4, 5))[i % 3]
            for k in range(8):
                em.emit("pe", (lambda k=k, wb=wb, bg_=bg_, wres=wres: lambda h: h.matmul(
                    psf[bg_][:, :], wb[:, k * 128:(k + 1) * 128], fm_xg[k][:],
                    start=(k == 0), stop=(k == 7)))(),
                    reads=(wres, f"fm_xg{k}"), writes=(f"bank{bg_}",),
                    tag=f"eg{grp}_{i}_{k}")
            for k in range(8):
                em.emit("pe", (lambda k=k, ub=ub, bu_=bu_, ures=ures: lambda h: h.matmul(
                    psf[bu_][:, :], ub[:, k * 128:(k + 1) * 128], fm_xg[k][:],
                    start=(k == 0), stop=(k == 7)))(),
                    reads=(ures, f"fm_xg{k}"), writes=(f"bank{bu_}",),
                    tag=f"eu{grp}_{i}_{k}")
            gsl, gslr = gsil[i % 2], f"gsil{i % 2}"
            em.emit("act", (lambda bg_=bg_, gsl=gsl: lambda h: h.activation(
                gsl[:], psf[bg_][:, :], AF.Silu))(),
                reads=(f"bank{bg_}",), writes=(gslr,), tag=f"es{grp}_{i}")
            em.emit("dve", (lambda i=i, bu_=bu_, gsl=gsl: lambda h: h.tensor_mul(
                guB[:, i * 512:(i + 1) * 512], psf[bu_][:, :], gsl[:]))(),
                reads=(f"bank{bu_}", gslr), writes=("guB",), tag=f"em{grp}_{i}")
        for tp in range(2):
            for i in range(32):
                db, dres = rdw[i % 3], f"rdw{i % 3}"
                em.emit(("sp", "act")[i % 2], (lambda i=i, db=db: lambda h: h.dma_start(
                    out=db[:], in_=rdb[i * 128:(i + 1) * 128, :]))(),
                    reads=("rdb",), writes=(dres,), dma=True, tag=f"lrd{grp}_{tp}_{i}")
                for t4 in (2 * tp, 2 * tp + 1):
                    b0 = 2 + 2 * (t4 % 2)
                    for nh in range(2):
                        em.emit("pe", (lambda i=i, t4=t4, nh=nh, db=db, b0=b0,
                                dres=dres: lambda h: h.matmul(
                                    psf[b0 + nh][:, :],
                                    guB[:, i * 512 + t4 * 128:i * 512 + (t4 + 1) * 128],
                                    db[:, nh * 512:(nh + 1) * 512],
                                    start=(i == 0), stop=(i == 31)))(),
                                reads=("guB", dres), writes=(f"bank{b0 + nh}",),
                                tag=f"ed{grp}_{t4}_{i}_{nh}")
            for t4 in (2 * tp, 2 * tp + 1):
                tile = grp * 4 + t4
                b0 = 2 + 2 * (t4 % 2)
                for nh in range(2):
                    em.emit("dve", (lambda tile=tile, nh=nh, b0=b0:
                            lambda h: h.tensor_scalar_mul(
                                ct_bf[:, nh * 512:(nh + 1) * 512],
                                psf[b0 + nh][:, :], wv_sb[:, tile:tile + 1]))(),
                            reads=(f"bank{b0 + nh}", "wv_sb"), writes=("ct_bf",),
                            tag=f"ece{grp}_{t4}_{nh}")
                em.emit("sp", (lambda tile=tile: lambda h: h.dma_start(
                    out=ctd[tile * 128:(tile + 1) * 128, :], in_=ct_bf[:]))(),
                    reads=("ct_bf",), writes=("ctd",), dma=True, tag=f"sct{tile}")

    for which, eqX, eres in ((0, eq1L, "eq1L"), (1, eq2L, "eq2L")):
        em.emit("dve", lambda h: h.memset(sc1[:, 0:32], 0.0), writes=("sc1",),
                tag=f"Re0_{which}")
        em.emit("dve", lambda h: h.memset(sc2[:, 0:32], 0.0), writes=("sc2",),
                tag=f"Rr0_{which}")
        for e_ in range(8):
            if e_ > 0:
                em.emit("dve", (lambda e_=e_, eqX=eqX: lambda h: h.scalar_tensor_tensor(
                    sc1[:, 0:32], eqX[:, e_::8], float(e_), sc1[:, 0:32],
                    ALU.mult, ALU.add))(),
                    reads=(eres, "sc1"), writes=("sc1",), tag=f"Re{which}_{e_}")
            em.emit("dve", (lambda e_=e_, eqX=eqX: lambda h: h.tensor_mul(
                sc3[:, 0:32], eqX[:, e_::8], rnkL[:, e_::8]))(),
                reads=(eres, "rnkL"), writes=("sc3",), tag=f"Rm{which}_{e_}")
            em.emit("dve", lambda h: h.tensor_add(sc2[:, 0:32], sc2[:, 0:32],
                                                  sc3[:, 0:32]),
                    reads=("sc2", "sc3"), writes=("sc2",), tag=f"Ra{which}_{e_}")
        em.emit("dve", lambda h: h.tensor_single_scalar(
            sc3[:, 0:32], sc2[:, 0:32], float(CAP), ALU.is_lt),
            reads=("sc2",), writes=("sc3",), tag=f"Rk{which}")
        em.emit("dve", lambda h: h.scalar_tensor_tensor(
            sc4[:, 0:32], sc1[:, 0:32], float(CAP), sc2[:, 0:32], ALU.mult, ALU.add),
            reads=("sc1", "sc2"), writes=("sc4",), tag=f"Rc{which}")
        em.emit("dve", lambda h: h.tensor_mul(
            sc4[:, 0:32], sc4[:, 0:32], sc3[:, 0:32]),
            reads=("sc4", "sc3"), writes=("sc4",), tag=f"Rb{which}")
        em.emit("dve", lambda h: h.tensor_scalar(
            sc2[:, 0:32], sc3[:, 0:32], -BIG, BIG, ALU.mult, ALU.add),
            reads=("sc3",), writes=("sc2",), tag=f"Rf{which}a")
        em.emit("dve", lambda h: h.tensor_add(
            sc4[:, 0:32], sc4[:, 0:32], sc2[:, 0:32]),
            reads=("sc4", "sc2"), writes=("sc4",), tag=f"Rf{which}")
        em.emit("dve", (lambda which=which: lambda h: h.tensor_copy(
            retoff_i[:, which::2], sc4[:, 0:32]))(),
            reads=("sc4",), writes=("retoff_i",), tag=f"Ri{which}")


    if debug:
        em.emit("act", lambda h: h.dma_start(out=dbg["ro"], in_=retoff_i[:]),
                reads=("retoff_i",), writes=("dbgout",), dma=True, tag="dbgro")

    em.emit("pool", lambda h: h.collective_compute(
        "AllGather", ALU.bypass, replica_groups=[list(range(8))],
        ins=[ctd.opt()], outs=[ct_all.opt()]),
        reads=("ctd",), writes=("ct_all",), cc=True, tag="AG2")

    if debug:
        em.emit("act", lambda h: h.dma_start(out=dbg["ct"], in_=ctd),
                reads=("ctd",), writes=("dbgout",), dma=True, tag="dbgct")

    sb_free("xg_tm0", "xg_tm1", *(f"fm_xg{k}" for k in range(8)), "guB",
            "rgw0", "rgw1", "rgw2", "ruw0", "ruw1", "ruw2",
            "rdw0", "rdw1", "rdw2", "gsil0", "gsil1", "ct_bf")
    em.barrier()

    xfg = sb("xfg", [128, 4 * D], F32)
    fm_xf = [sb(f"fm_xf{k}", [128, 512], BF16) for k in range(8)]
    guC = sb("guC", [128, 64 * 512], BF16)
    sgw = [sb(f"sgw{i}", [128, 1024], BF16) for i in range(3)]
    suw = [sb(f"suw{i}", [128, 1024], BF16) for i in range(3)]
    sdw = [sb(f"sdw{i}", [128, 1024], BF16) for i in range(3)]
    gsil2 = [sb(f"gsil2_{i}", [128, 512], BF16) for i in range(2)]
    sh_sb = [sb(f"sh_sb{i}", [128, D], F32) for i in range(2)]
    xw = [sb(f"xw{i}", [128, D], F32) for i in range(2)]
    g0 = [sb(f"g0_{i}", [128, D], BF16) for i in range(2)]
    g1 = [sb(f"g1_{i}", [128, D], BF16) for i in range(2)]
    yw = [sb(f"yw{i}", [128, D], F32) for i in range(2)]

    for G in range(NGROUP):
        em.emit("pool", (lambda G=G: lambda h: h.dma_start(
            out=xfg[:].rearrange("p (j d) -> p j d", d=D),
            in_=xfd[G * 512:(G + 1) * 512, :].rearrange(
                "(j p) d -> p j d", p=128)))(),
            reads=("xfd",), writes=("xfg",), dma=True, tag=f"lxf{G}")
        for j in range(4):
            for k in range(8):
                tb = k % 2
                em.emit("pe", (lambda k=k, j=j, tb=tb: lambda h: h.transpose(
                    psf[tb][:, 128:256], xfg[:, j * D + k * 128:j * D + (k + 1) * 128],
                    cst["identf"][:]))(),
                    reads=("xfg", "identf"), writes=(f"bank{tb}",), tag=f"xft{G}_{j}_{k}")
                em.emit("dve", (lambda k=k, j=j, tb=tb: lambda h: h.tensor_copy(
                    fm_xf[k][:, j * 128:(j + 1) * 128], psf[tb][:, 128:256]))(),
                    reads=(f"bank{tb}",), writes=(f"fm_xf{k}",), tag=f"xfte{G}_{j}_{k}")
        for i in range(64):
            wb, wres = sgw[i % 3], f"sgw{i % 3}"
            ub, ures = suw[i % 3], f"suw{i % 3}"
            em.emit("sp", (lambda i=i, wb=wb: lambda h: h.dma_start(
                out=wb[:], in_=sgb[i].rearrange("p dk f -> p (dk f)")))(),
                reads=("sgb",), writes=(wres,), dma=True, tag=f"lsg{G}_{i}")
            em.emit("act", (lambda i=i, ub=ub: lambda h: h.dma_start(
                out=ub[:], in_=sub[i].rearrange("p dk f -> p (dk f)")))(),
                reads=("sub",), writes=(ures,), dma=True, tag=f"lsu{G}_{i}")
            bg_, bu_ = ((0, 1), (2, 3), (4, 5))[i % 3]
            for k in range(8):
                em.emit("pe", (lambda k=k, wb=wb, bg_=bg_, wres=wres: lambda h: h.matmul(
                    psf[bg_][:, :], wb[:, k * 128:(k + 1) * 128], fm_xf[k][:],
                    start=(k == 0), stop=(k == 7)))(),
                    reads=(wres, f"fm_xf{k}"), writes=(f"bank{bg_}",),
                    tag=f"sg{G}_{i}_{k}")
            for k in range(8):
                em.emit("pe", (lambda k=k, ub=ub, bu_=bu_, ures=ures: lambda h: h.matmul(
                    psf[bu_][:, :], ub[:, k * 128:(k + 1) * 128], fm_xf[k][:],
                    start=(k == 0), stop=(k == 7)))(),
                    reads=(ures, f"fm_xf{k}"), writes=(f"bank{bu_}",),
                    tag=f"su{G}_{i}_{k}")
            gsl, gslr = gsil2[i % 2], f"gsil2_{i % 2}"
            em.emit("act", (lambda bg_=bg_, gsl=gsl: lambda h: h.activation(
                gsl[:], psf[bg_][:, :], AF.Silu))(),
                reads=(f"bank{bg_}",), writes=(gslr,), tag=f"ss{G}_{i}")
            em.emit("dve", (lambda i=i, bu_=bu_, gsl=gsl: lambda h: h.tensor_mul(
                guC[:, i * 512:(i + 1) * 512], psf[bu_][:, :], gsl[:]))(),
                reads=(f"bank{bu_}", gslr), writes=("guC",), tag=f"sm{G}_{i}")
        for tp in range(2):
            for i in range(64):
                db, dres = sdw[i % 3], f"sdw{i % 3}"
                em.emit(("sp", "act")[i % 2], (lambda i=i, db=db: lambda h: h.dma_start(
                    out=db[:], in_=sdb[i * 128:(i + 1) * 128, :]))(),
                    reads=("sdb",), writes=(dres,), dma=True, tag=f"lsd{G}_{tp}_{i}")
                for t4 in (2 * tp, 2 * tp + 1):
                    b0 = 2 + 2 * (t4 % 2)
                    for nh in range(2):
                        em.emit("pe", (lambda i=i, t4=t4, nh=nh, db=db, b0=b0,
                                dres=dres: lambda h: h.matmul(
                                    psf[b0 + nh][:, :],
                                    guC[:, i * 512 + t4 * 128:i * 512 + (t4 + 1) * 128],
                                    db[:, nh * 512:(nh + 1) * 512],
                                    start=(i == 0), stop=(i == 63)))(),
                                reads=("guC", dres), writes=(f"bank{b0 + nh}",),
                                tag=f"sd{G}_{t4}_{i}_{nh}")
            for t4 in (2 * tp, 2 * tp + 1):
                tile = G * 4 + t4
                b0 = 2 + 2 * (t4 % 2)
                shb, shr = sh_sb[t4 % 2], f"sh_sb{t4 % 2}"
                for nh in range(2):
                    em.emit("dve", (lambda nh=nh, b0=b0, shb=shb: lambda h: h.tensor_copy(
                        shb[:, nh * 512:(nh + 1) * 512], psf[b0 + nh][:, :]))(),
                        reads=(f"bank{b0 + nh}",), writes=(shr,),
                        tag=f"sce{G}_{t4}_{nh}")
                if debug:
                    em.emit("sp", (lambda tile=tile, shb=shb: lambda h: h.dma_start(
                        out=shd[tile * 128:(tile + 1) * 128, :], in_=shb[:]))(),
                        reads=(shr,), writes=("shd",), dma=True, tag=f"ssh{tile}")
                p = t4 % 2
                em.emit("sp", (lambda tile=tile, p=p: lambda h: h.dma_start(
                    out=xw[p][:], in_=x1d[tile * 128:(tile + 1) * 128, :]))(),
                    reads=("x1d",), writes=(f"xw{p}",), dma=True, tag=f"fx{tile}")
                em.emit("dve", (lambda p=p: lambda h: h.memset(g0[p][:], 0.0))(),
                        writes=(f"g0_{p}",), tag=f"fz0{tile}")
                em.emit("dve", (lambda p=p: lambda h: h.memset(g1[p][:], 0.0))(),
                        writes=(f"g1_{p}",), tag=f"fz1{tile}")
                em.emit("pool", (lambda g=tile, p=p: lambda h: h.indirect_dma_start(
                    out=g0[p][:], out_offset=None, in_=ct_all,
                    in_offset=bass.IndirectOffsetOnAxis(
                        retoff_i[:, 2 * g:2 * g + 1], axis=0),
                    bounds_check=regs["ct"], oob_is_err=False))(),
                    reads=("ct_all", "retoff_i", f"g0_{p}"), writes=(f"g0_{p}",),
                    dma=True, tag=f"fg0{tile}")
                em.emit("pool", (lambda g=tile, p=p: lambda h: h.indirect_dma_start(
                    out=g1[p][:], out_offset=None, in_=ct_all,
                    in_offset=bass.IndirectOffsetOnAxis(
                        retoff_i[:, 2 * g + 1:2 * g + 2], axis=0),
                    bounds_check=regs["ct"], oob_is_err=False))(),
                    reads=("ct_all", "retoff_i", f"g1_{p}"), writes=(f"g1_{p}",),
                    dma=True, tag=f"fg1{tile}")
                em.emit("dve", (lambda p=p, shb=shb: lambda h: h.tensor_add(
                    yw[p][:], xw[p][:], shb[:]))(),
                    reads=(f"xw{p}", shr), writes=(f"yw{p}",), tag=f"fa{tile}")
                em.emit("dve", (lambda p=p: lambda h: h.tensor_add(
                    yw[p][:], yw[p][:], g0[p][:]))(),
                    reads=(f"yw{p}", f"g0_{p}"), writes=(f"yw{p}",), tag=f"fb{tile}")
                em.emit("dve", (lambda p=p: lambda h: h.tensor_add(
                    yw[p][:], yw[p][:], g1[p][:]))(),
                    reads=(f"yw{p}", f"g1_{p}"), writes=(f"yw{p}",), tag=f"fc{tile}")
                em.emit("sp", (lambda tile=tile, p=p: lambda h: h.dma_start(
                    out=y_out[tile * 128:(tile + 1) * 128, :], in_=yw[p][:]))(),
                    reads=(f"yw{p}",), writes=("y",), dma=True, tag=f"fy{tile}")

    if debug:
        em.emit("act", lambda h: h.dma_start(out=dbg["sh"], in_=shd),
                reads=("shd",), writes=("dbgout",), dma=True, tag="dbgsh")

    em.barrier()
    em.finalize()

    def make_block():
        with nc.Block() as block:
            @block.tensor
            def _(t):
                em.replay("pe", t)

            @block.vector
            def _(v):
                em.replay("dve", v)

            @block.scalar
            def _(s):
                em.replay("act", s)

            @block.gpsimd
            def _(g_):
                regs["xg"] = g_.to_reg(8 * T_LOC - 1)
                regs["ct"] = g_.to_reg(8 * CAP - 1)
                em.replay("pool", g_)

            @block.sync
            def _(sy):
                em.replay("sp", sy)
    make_block()
    return nc


def make_consts():
    c = {}
    c["identb"] = np.eye(128, dtype=BF16NP)
    c["identf"] = np.eye(128, dtype=np.float32)
    c["utmask"] = np.triu(np.ones((128, 128), np.float32))
    c["ltstrict"] = np.triu(np.ones((128, 128), np.float32), 1)
    pp, qq = np.meshgrid(np.arange(128), np.arange(128), indexing="ij")
    c["lt8"] = (((pp % 8) == (qq % 8)) & ((pp // 8) < (qq // 8))).astype(np.float32)
    c["cs8"] = ((pp % 8) == (qq % 8)).astype(np.float32)
    c["iota_f128"] = np.broadcast_to(np.arange(128, dtype=np.float32),
                                     (128, 128)).copy()
    c["iota_f40"] = np.broadcast_to(np.arange(NTC, dtype=np.float32),
                                    (128, NTC)).copy()
    c["iota_p"] = np.arange(128, dtype=np.float32).reshape(128, 1).copy()
    c["tok_pg"] = (np.arange(256, dtype=np.float32)[None, :] * 128
                   + np.arange(128, dtype=np.float32)[:, None]).copy()
    c["ones_col"] = np.ones((128, 1), np.float32)
    c["ones_row"] = np.ones((1, 128), np.float32)
    return c


_CACHED = {}


def kernel(L0_attn_x, L0_attn_W1, L0_attn_b1, L0_attn_W2, L0_attn_b2,
           L0_ffn_router_w, L0_ffn_rg, L0_ffn_ru, L0_ffn_rd,
           L0_ffn_sg, L0_ffn_su, L0_ffn_sd, _debug=False):
    if ("nc", _debug) not in _CACHED:
        _CACHED[("nc", _debug)] = build_program(debug=_debug)
    nc = _CACHED[("nc", _debug)]

    x = np.ascontiguousarray(np.asarray(L0_attn_x, dtype=np.float32))
    W1 = np.ascontiguousarray(np.asarray(L0_attn_W1, dtype=np.float32))
    b1 = np.asarray(L0_attn_b1, dtype=np.float32)
    W2 = np.ascontiguousarray(np.asarray(L0_attn_W2, dtype=np.float32))
    b2 = np.asarray(L0_attn_b2, dtype=np.float32)
    rw = np.asarray(L0_ffn_router_w, dtype=np.float32)
    rg = np.asarray(L0_ffn_rg, dtype=np.float32)
    ru = np.asarray(L0_ffn_ru, dtype=np.float32)
    rd = np.asarray(L0_ffn_rd, dtype=np.float32)
    sg = np.ascontiguousarray(np.asarray(L0_ffn_sg, dtype=np.float32))
    su = np.ascontiguousarray(np.asarray(L0_ffn_su, dtype=np.float32))
    sd = np.ascontiguousarray(np.asarray(L0_ffn_sd, dtype=np.float32))

    consts = make_consts()
    b1c = np.ascontiguousarray(b1.reshape(24, 128).T)
    b1kv = np.ascontiguousarray(np.broadcast_to(
        np.concatenate([b1[0:D], b1[2 * D:3 * D]]), (128, 2 * D)))
    b2bc = np.ascontiguousarray(np.broadcast_to(b2, (128, D)))
    rwT = np.ascontiguousarray(rw.T)

    in_maps = []
    for c in range(8):
        csel = np.zeros((128, E), np.float32)
        csel[:, c] = 1.0
        pm = np.zeros((E, 1), np.float32)
        pm[:c, 0] = 1.0
        m = dict(x=np.ascontiguousarray(x[c]), W1=W1, b1c=b1c, b1kv=b1kv,
                 W2=W2, b2bc=b2bc, rwT=rwT,
                 rg=np.ascontiguousarray(rg[c]), ru=np.ascontiguousarray(ru[c]),
                 rd=np.ascontiguousarray(rd[c]), sg=sg, su=su, sd=sd,
                 csel=csel, prevmask=pm, **consts)
        in_maps.append(m)
    res = run_bass_kernel_spmd(nc, in_maps, core_ids=list(range(8)))
    if _debug:
        return res
    return np.stack([np.asarray(r["y"], dtype=np.float32) for r in res.results],
                    axis=0)


# version 05
# speedup vs baseline: 1.0503x; 1.0503x over previous
import numpy as np
import ml_dtypes
import concourse.bass as bass
import concourse.mybir as mybir
from concourse.bass_utils import run_bass_kernel_spmd

F32 = mybir.dt.float32
BF16 = mybir.dt.bfloat16
I32 = mybir.dt.int32
AF = mybir.ActivationFunctionType
ALU = mybir.AluOpType
AX = mybir.AxisListType
BF16NP = ml_dtypes.bfloat16

T_LOC = 4096
D = 1024
NH = 16
NCHUNK = 32
NGROUP = 8
E = 8
CAP = 5120
NTC = CAP // 128
I_R = 4096
I_S = 8192
EPS_LN = 1e-5
EPS_RMS = float(np.finfo(np.float32).eps)
BIG = 1.0e9

ENGS = ("pe", "dve", "act", "pool", "sp")
DMAQ = ("sp", "act", "pool")


class Emitter:

    def __init__(self, nc):
        self.nc = nc
        self.ops = {e: [] for e in ENGS}
        self.last_writer = {}
        self.readers = {}
        self.dsem = {}
        self.cc_n = 0
        self.esem = {e: nc.semaphore(f"esem_{e}").__enter__() for e in ENGS}
        self.ccsem = nc.semaphore("ccsem").__enter__()
        self.barsem = nc.semaphore("barsem").__enter__()
        self.bar_n = 0

    def _dep(self, waits, ev):
        if ev is None:
            return
        if ev[0] == "cc":
            key, val = ("cc",), ev[1]
        else:
            key, val = (ev[0], ev[1]), ev[2]
        if waits.get(key, -1) < val:
            waits[key] = val

    def emit(self, eng, fn, reads=(), writes=(), dma=False, cc=False, tag=""):
        waits = {}
        for r in reads:
            self._dep(waits, self.last_writer.get(r))
        for w in writes:
            self._dep(waits, self.last_writer.get(w))
            for ev in self.readers.get(w, ()):
                self._dep(waits, ev)
        idx = len(self.ops[eng])
        sem_key = None
        if dma:
            assert eng in DMAQ and len(writes) >= 1
            sem_key = writes[0]
            if sem_key not in self.dsem:
                self.dsem[sem_key] = [
                    self.nc.semaphore(f"ds_{len(self.dsem)}").__enter__(), 0, eng]
            rec = self.dsem[sem_key]
            rec[1] += 16
            rec[2] = eng
            ev = ("dma", sem_key, rec[1])
        elif cc:
            self.cc_n += 1
            ev = ("cc", self.cc_n)
        else:
            ev = ("eng", eng, idx)
        if eng == "pe":
            waits = {k: v for k, v in waits.items()
                     if not (k[0] == "eng" and k[1] == "pe")}
        self.ops[eng].append(dict(fn=fn, waits=waits, dma=dma, cc=cc,
                                  tag=tag, need_inc=False, is_bar=False,
                                  sem_key=sem_key))
        for r in reads:
            self.readers.setdefault(r, []).append(ev)
        for w in writes:
            self.last_writer[w] = ev
            self.readers[w] = []
        return ev

    def drain(self, q):
        pend = [(rec[0], rec[1]) for rec in self.dsem.values() if rec[2] == q]

        def fn(h):
            for sem, n in pend:
                h.wait_ge(sem, n)
            return h.nop()
        self.ops[q].append(dict(fn=fn, waits={}, dma=False, cc=False,
                                tag=f"drain_{q}", need_inc=False, is_bar=False,
                                sem_key=None))

    def barrier(self):
        self.bar_n += 1
        target = 5 * self.bar_n
        pend = {q: [(rec[0], rec[1]) for rec in self.dsem.values()
                    if rec[2] == q] for q in DMAQ}

        def mk(eng):
            def fn(h):
                for sem, n in pend.get(eng, ()):
                    h.wait_ge(sem, n)
                h.nop().then_inc(self.barsem, 1)
                h.wait_ge(self.barsem, target)
            return fn
        for e in ENGS:
            self.ops[e].append(dict(fn=mk(e), waits={}, dma=False, cc=False,
                                    tag=f"bar{self.bar_n}", need_inc=False,
                                    is_bar=True, sem_key=None))
        self.last_writer = {k: v for k, v in self.last_writer.items()
                            if v[0] == "cc"}
        self.readers = {}

    def finalize(self):
        for e in ENGS:
            for op in self.ops[e]:
                for key, val in op["waits"].items():
                    if key[0] == "eng":
                        self.ops[key[1]][val]["need_inc"] = True
        for e in ENGS:
            c = 0
            for op in self.ops[e]:
                if op["need_inc"]:
                    c += 1
                op["inc_no"] = c

    def replay(self, eng, h):
        waited = {}
        for op in self.ops[eng]:
            if op["is_bar"]:
                op["fn"](h)
                waited = {}
                continue
            for key, val in sorted(op["waits"].items(), key=lambda kv: str(kv[0])):
                if key[0] == "eng":
                    sem = self.esem[key[1]]
                    target = self.ops[key[1]][val]["inc_no"]
                elif key[0] == "dma":
                    sem, target = self.dsem[key[1]][0], val
                else:
                    sem, target = self.ccsem, val
                if waited.get(key, -1) < target:
                    h.wait_ge(sem, target)
                    waited[key] = target
            ins = op["fn"](h)
            if op["dma"]:
                ins.then_inc(self.dsem[op["sem_key"]][0], 16)
            elif op["cc"]:
                ins.then_inc(self.ccsem, 1)
            elif op["need_inc"]:
                assert ins is not None, f"op {op['tag']} needs inc, returned None"
                ins.then_inc(self.esem[eng], 1)


def build_program(debug=False):
    nc = bass.Bass("TRN2", target_bir_lowering=False)

    def din(name, shape, dt=F32):
        return nc.dram_tensor(name, shape, dt, kind="ExternalInput").ap()

    def dout(name, shape, dt=F32):
        return nc.dram_tensor(name, shape, dt, kind="ExternalOutput").ap()

    def dint(name, shape, dt):
        return nc.dram_tensor(name, shape, dt, kind="Internal").ap()

    x_in = din("x", [T_LOC, D])
    W1 = din("W1", [D, 3 * D])
    b1c = din("b1c", [128, 24])
    b1kv = din("b1kv", [128, 2 * D])
    W2 = din("W2", [D, D])
    b2bc = din("b2bc", [128, D])
    rwT = din("rwT", [D, E])
    rg = din("rg", [D, I_R])
    ru = din("ru", [D, I_R])
    rd = din("rd", [I_R, D])
    sg = din("sg", [D, I_S])
    su = din("su", [D, I_S])
    sd = din("sd", [I_S, D])
    identb = din("identb", [128, 128], BF16)
    identf = din("identf", [128, 128])
    utmask = din("utmask", [128, 128])
    ltstrict = din("ltstrict", [128, 128])
    lt8 = din("lt8", [128, 128])
    cs8 = din("cs8", [128, 128])
    iota_f128 = din("iota_f128", [128, 128])
    iota_f40 = din("iota_f40", [128, NTC])
    iota_p = din("iota_p", [128, 1])
    tok_pg = din("tok_pg", [128, 256])
    ones_col = din("ones_col", [128, 1])
    ones_row = din("ones_row", [1, 128])
    csel = din("csel", [128, E])
    prevmask = din("prevmask", [E, 1])
    y_out = dout("y", [T_LOC, D])

    rgb = dint("rgb", [I_R // 128, 128, 8, 128], BF16)
    rub = dint("rub", [I_R // 128, 128, 8, 128], BF16)
    rdb = dint("rdb", [I_R, D], BF16)
    sgb = dint("sgb", [I_S // 128, 128, 8, 128], BF16)
    sub = dint("sub", [I_S // 128, 128, 8, 128], BF16)
    sdb = dint("sdb", [I_S, D], BF16)
    x1d = dint("x1d", [T_LOC, D], F32)
    xfd = dint("xfd", [T_LOC, D], BF16)
    lgd = dint("lgd", [129, 256], F32)
    ctd = dint("ctd", [CAP, D], BF16)
    shd = dint("shd", [T_LOC, D], F32)
    xf_all = dint("xf_all", [8 * T_LOC, D], BF16)
    lg_all = dint("lg_all", [8 * 129, 256], F32)
    ct_all = dint("ct_all", [8 * CAP, D], BF16)

    if debug:
        dbg = {nm: dout("dbg_" + nm, shp, dt) for nm, shp, dt in (
            ("x1", [T_LOC, D], F32), ("xf", [T_LOC, D], BF16),
            ("lg", [129, 256], F32), ("disp", [128, NTC], I32),
            ("wv", [128, NTC], F32), ("ct", [CAP, D], BF16),
            ("sh", [T_LOC, D], F32), ("ro", [128, NCHUNK * 2], I32),
            ("rnk", [128, 256], F32), ("sel", [128, 256], F32),
            ("wdc", [128, 256], F32), ("xln", [T_LOC, D], F32),
            ("ktm", [T_LOC, D], BF16), ("vtm", [T_LOC, D], BF16),
            ("kq", [NGROUP, 16, 128, 512], BF16),
            ("am", [NGROUP, 8, 128, 512], BF16))}

    em = Emitter(nc)
    regs = {}
    guards = []

    def sb(name, shape, dt):
        g = nc.sbuf_tensor(name, shape, dt)
        h = g.__enter__()
        guards.append((name, g))
        return h

    def sb_free(*names):
        want = set(names)
        while want:
            n, g = guards[-1]
            assert n in want, f"sb_free non-tail: {n} not in requested set"
            g.__exit__(None, None, None)
            guards.pop()
            want.discard(n)

    psf = [nc.psum_tensor(f"psf{i}", [128, 512], F32).__enter__() for i in range(7)]
    psb = nc.psum_tensor("psb", [128, 1024], BF16).__enter__()
    BKB = "bank_bf"

    cst = {}
    for nm, ap_, shape, dt in (
        ("identb", identb, [128, 128], BF16), ("identf", identf, [128, 128], F32),
        ("utmask", utmask, [128, 128], F32), ("ltstrict", ltstrict, [128, 128], F32),
        ("lt8", lt8, [128, 128], F32), ("cs8", cs8, [128, 128], F32),
        ("iota_f128", iota_f128, [128, 128], F32), ("iota_f40", iota_f40, [128, NTC], F32),
        ("iota_p", iota_p, [128, 1], F32), ("tok_pg", tok_pg, [128, 256], F32),
        ("ones_col", ones_col, [128, 1], F32),
        ("ones_row", ones_row, [1, 128], F32), ("csel", csel, [128, E], F32),
        ("prevmask", prevmask, [E, 1], F32), ("b1c", b1c, [128, 24], F32),
        ("b1kv", b1kv, [128, 2 * D], F32), ("b2bc", b2bc, [128, D], F32),
    ):
        cst[nm] = sb("c_" + nm, shape, dt)
        em.emit("sp", (lambda a=ap_, s=cst[nm]: lambda h: h.dma_start(out=s[:], in_=a))(),
                writes=(nm,), dma=True, tag=f"ld_{nm}")
    rwb = sb("rwb", [128, 8 * E], BF16)
    em.emit("pool", lambda h: h.dma_start(
        out=rwb[:].rearrange("p (dk e) -> p dk e", e=E),
        in_=rwT.rearrange("(dk p) e -> p dk e", p=128)),
        writes=("rwb",), dma=True, tag="ld_rwb")

    lg_sb = sb("lg_sb", [128, 256], F32)
    eq1L = sb("eq1L", [128, 256], F32)
    eq2L = sb("eq2L", [128, 256], F32)
    rnkL = sb("rnkL", [128, 256], F32)
    retoff_i = sb("retoff_i", [128, NCHUNK * 2], I32)
    disp_i = sb("disp_i", [128, NTC], I32)
    wv_sb = sb("wv_sb", [128, NTC], F32)
    sc1 = sb("sc1", [128, 256], F32)
    sc2 = sb("sc2", [128, 256], F32)
    sc3 = sb("sc3", [128, 256], F32)
    sc4 = sb("sc4", [128, 256], F32)
    srow = sb("srow", [128, 16], F32)

    W1b = [sb(f"W1b{k}", [128, 3 * D], BF16) for k in range(8)]
    W2b = [sb(f"W2b{k}", [128, D], BF16) for k in range(8)]
    for k in range(8):
        em.emit("pool", (lambda k=k: lambda h: h.dma_start(
            out=W1b[k][:], in_=W1[k * 128:(k + 1) * 128, :]))(),
            writes=(f"W1b{k}",), dma=True, tag=f"ld_W1b{k}")
        em.emit("pool", (lambda k=k: lambda h: h.dma_start(
            out=W2b[k][:], in_=W2[k * 128:(k + 1) * 128, :]))(),
            writes=(f"W2b{k}",), dma=True, tag=f"ld_W2b{k}")

    em.drain("pool")
    em.emit("pool", lambda h: h.dma_start(out=rdb[:, :], in_=rd[:, :]),
            writes=("rdb",), dma=True, tag="cast_rd")
    for src, dst, nm in ((rg, rgb, "rgb"), (ru, rub, "rub"),
                         (sg, sgb, "sgb"), (su, sub, "sub")):
        for dk in range(8):
            em.emit("pool", (lambda s=src, d=dst, dk=dk: lambda h: h.dma_start(
                out=d[:, :, dk, :],
                in_=s[dk * 128:(dk + 1) * 128, :].rearrange(
                    "p (i f) -> i p f", f=128)))(),
                writes=(nm,), dma=True, tag=f"cast_{nm}{dk}")
    em.emit("pool", lambda h: h.dma_start(out=sdb[:, :], in_=sd[:, :]),
            writes=("sdb",), dma=True, tag="cast_sd")


    x_tm = [sb(f"x_tm{i}", [128, 4 * D], F32) for i in range(2)]
    xln_tm = sb("xln_tm", [128, D], F32)
    fm_xln = [sb(f"fm_xln{k}", [128, 512], BF16) for k in range(8)]
    fm_kq = [sb(f"fm_kq{m}", [128, 512], BF16) for m in range(16)]
    ktm2 = [sb(f"ktm{i}", [128, D], BF16) for i in range(2)]
    vtm2 = [sb(f"vtm{i}", [128, D], BF16) for i in range(2)]
    a_sb2 = [sb(f"a_sb{i}", [128, 128], BF16) for i in range(2)]
    s_f32 = sb("s_f32", [128, 512], F32)
    s_bf = sb("s_bf", [128, 512], BF16)
    fm_am = [sb(f"fm_am{k}", [128, 512], BF16) for k in range(8)]
    x1_tm = sb("x1_tm", [128, D], F32)
    xf_tm = sb("xf_tm", [128, D], BF16)
    xf32 = sb("xf32", [128, D], F32)
    fm_xfc = [sb(f"fm_xfc{k}", [128, 128], BF16) for k in range(8)]
    stat2 = [sb(f"stat{i}", [128, 8], F32) for i in range(2)]
    x2s2 = [sb(f"x2s{i}", [128, D], F32) for i in range(2)]

    em.emit("dve", lambda h: h.memset(s_f32[:], 0.0), writes=("s_f32",), tag="s0")
    em.emit("dve", lambda h: h.memset(s_bf[:], 0.0), writes=("s_bf",), tag="s0b")

    def ln_apply(xin_ap, xout_ap, res_in, res_out, eps, tag, par=0):
        stat, x2s = stat2[par], x2s2[par]
        statr, x2sr = f"stat{par}", f"x2s{par}"
        snm, sv, sr = (stat[:, i:i + 1] for i in range(3))
        em.emit("dve", lambda h: h.reduce_sum(snm, xin_ap, axis=AX.X),
                reads=(res_in,), writes=(statr,), tag=f"{tag}a")
        em.emit("dve", lambda h: h.tensor_mul(x2s[:], xin_ap, xin_ap),
                reads=(res_in,), writes=(x2sr,), tag=f"{tag}b")
        em.emit("dve", lambda h: h.reduce_sum(sv, x2s[:], axis=AX.X),
                reads=(x2sr,), writes=(statr,), tag=f"{tag}c")
        em.emit("dve", lambda h: h.tensor_scalar_mul(snm, snm, -1.0 / D),
                reads=(statr,), writes=(statr,), tag=f"{tag}d")
        em.emit("dve", lambda h: h.tensor_scalar(sv, sv, 1.0 / D, float(eps),
                                                 ALU.mult, ALU.add),
                reads=(statr,), writes=(statr,), tag=f"{tag}e")
        em.emit("dve", lambda h: h.scalar_tensor_tensor(sr, snm, -1.0, snm,
                                                        ALU.mult, ALU.mult),
                reads=(statr,), writes=(statr,), tag=f"{tag}f")
        em.emit("dve", lambda h: h.tensor_sub(sv, sv, sr),
                reads=(statr,), writes=(statr,), tag=f"{tag}g")
        em.emit("act", lambda h: h.sqrt(sr, sv),
                reads=(statr,), writes=(statr,), tag=f"{tag}h")
        em.emit("dve", lambda h: h.reciprocal(sr, sr),
                reads=(statr,), writes=(statr,), tag=f"{tag}i")
        tnw = stat[:, 3:4]
        em.emit("dve", lambda h: h.tensor_mul(tnw, sr, sr),
                reads=(statr,), writes=(statr,), tag=f"{tag}n1")
        em.emit("dve", lambda h: h.tensor_mul(tnw, tnw, sv),
                reads=(statr,), writes=(statr,), tag=f"{tag}n2")
        em.emit("dve", lambda h: h.tensor_scalar(tnw, tnw, -0.5, 1.5,
                                                 ALU.mult, ALU.add),
                reads=(statr,), writes=(statr,), tag=f"{tag}n3")
        em.emit("dve", lambda h: h.tensor_mul(sr, sr, tnw),
                reads=(statr,), writes=(statr,), tag=f"{tag}n4")
        em.emit("dve", lambda h: h.tensor_scalar(xout_ap, xin_ap, snm, sr,
                                                 ALU.add, ALU.mult),
                reads=(res_in, statr), writes=(res_out,), tag=f"{tag}k")

    for G in range(NGROUP):
        xb, xres = x_tm[G % 2], f"x_tm{G % 2}"
        em.emit("sp", (lambda G=G, xb=xb: lambda h: h.dma_start(
            out=xb[:].rearrange("p (j d) -> p j d", d=D),
            in_=x_in[G * 512:(G + 1) * 512, :].rearrange(
                "(j p) d -> p j d", p=128)))(),
            writes=(xres,), dma=True, tag=f"ldx{G}")
        for j in range(4):
            xj = xb[:, j * D:(j + 1) * D]
            ln_apply(xj, xln_tm[:], xres, "xln_tm", EPS_LN, f"l1_{G}_{j}", par=j % 2)
            if debug:
                em.emit("act", (lambda G=G, j=j: lambda h: h.dma_start(
                    out=dbg["xln"][(G * 4 + j) * 128:(G * 4 + j + 1) * 128, :],
                    in_=xln_tm[:]))(),
                    reads=("xln_tm",), writes=("dbgout",), dma=True, tag=f"dxln{G}_{j}")
            for k in range(8):
                bk = k % 2
                em.emit("pe", (lambda k=k, bk=bk: lambda h: h.transpose(
                    psf[bk][:, 128:256], xln_tm[:, k * 128:(k + 1) * 128],
                    cst["identf"][:]))(),
                    reads=("xln_tm", "identf"), writes=(f"bank{bk}",),
                    tag=f"trx{G}_{j}_{k}")
                em.emit("dve", (lambda k=k, j=j, bk=bk: lambda h: h.tensor_copy(
                    fm_xln[k][:, j * 128:(j + 1) * 128], psf[bk][:, 128:256]))(),
                    reads=(f"bank{bk}",), writes=(f"fm_xln{k}",),
                    tag=f"trxe{G}_{j}_{k}")
        for m in range(16):
            bk = m % 2
            for k in range(8):
                em.emit("pe", (lambda m=m, k=k, bk=bk: lambda h: h.matmul(
                    psf[bk][:, :], W1b[k][:, m * 128:(m + 1) * 128], fm_xln[k][:],
                    start=(k == 0), stop=(k == 7)))(),
                    reads=(f"W1b{k}", f"fm_xln{k}"), writes=(f"bank{bk}",),
                    tag=f"a4_{G}_{m}_{k}")
            em.emit("dve", (lambda m=m, bk=bk: lambda h: h.tensor_scalar(
                fm_kq[m][:], psf[bk][:, :], cst["b1c"][:, m:m + 1], None, ALU.add))(),
                reads=(f"bank{bk}", "b1c"), writes=(f"fm_kq{m}",), tag=f"a4e_{G}_{m}")
            if debug:
                em.emit("act", (lambda G=G, m=m: lambda h: h.dma_start(
                    out=dbg["kq"][G, m], in_=fm_kq[m][:]))(),
                    reads=(f"fm_kq{m}",), writes=("dbgout",), dma=True, tag=f"dkq{G}_{m}")
        for j in range(4):
            g = G * 4 + j
            for piece in range(4):
                col0 = (0 if piece < 2 else 2 * D) + (piece % 2) * 512
                for k in range(8):
                    em.emit("pe", (lambda k=k, piece=piece, col0=col0, j=j:
                            lambda h: h.matmul(
                                psf[piece][:, :], fm_xln[k][:, j * 128:(j + 1) * 128],
                                W1b[k][:, col0:col0 + 512],
                                start=(k == 0), stop=(k == 7)))(),
                            reads=(f"fm_xln{k}", f"W1b{k}"), writes=(f"bank{piece}",),
                            tag=f"a5_{g}_{piece}_{k}")
            ktm, vtm = ktm2[j % 2], vtm2[j % 2]
            kres, vres = f"ktm{j % 2}", f"vtm{j % 2}"
            for piece in range(4):
                dst, dres = (ktm, kres) if piece < 2 else (vtm, vres)
                boff = (0 if piece < 2 else D) + (piece % 2) * 512
                em.emit("dve", (lambda dst=dst, piece=piece, boff=boff:
                        lambda h: h.tensor_add(
                            dst[:, (piece % 2) * 512:(piece % 2) * 512 + 512],
                            psf[piece][:, :], cst["b1kv"][:, boff:boff + 512]))(),
                        reads=(f"bank{piece}", "b1kv"), writes=(dres,),
                        tag=f"a5e_{g}_{piece}")
            for h_ in range(NH):
                bh = (h_ % 2) * 64
                mq, mk_ = 8 + h_ // 2, h_ // 2
                bnk = 4 if h_ % 2 == 0 else 2
                a_sb, ares = a_sb2[h_ % 2], f"a_sb{h_ % 2}"
                em.emit("pe", (lambda h_=h_, j=j, bh=bh, mq=mq, mk_=mk_, bnk=bnk:
                        lambda h: h.matmul(
                            psf[bnk][:, 0:128],
                            fm_kq[mk_][bh:bh + 64, j * 128:(j + 1) * 128],
                            fm_kq[mq][bh:bh + 64, j * 128:(j + 1) * 128],
                            start=True, stop=True))(),
                        reads=(f"fm_kq{mk_}", f"fm_kq{mq}"), writes=(f"bank{bnk}",),
                        tag=f"at{g}_{h_}")
                em.emit("dve", (lambda bnk=bnk, a_sb=a_sb: lambda h: h.tensor_mul(
                    a_sb[:], psf[bnk][:, 0:128], cst["utmask"][:]))(),
                    reads=(f"bank{bnk}", "utmask"), writes=(ares,),
                    tag=f"atm{g}_{h_}")
                em.emit("pe", (lambda h_=h_, bh=bh, bnk=bnk, g=g, vtm=vtm,
                        a_sb=a_sb: lambda h: h.matmul(
                    psf[bnk][bh:bh + 64, 128:256], vtm[:, h_ * 64:(h_ + 1) * 64],
                    a_sb[:], start=True, stop=(g == 0)))(),
                    reads=(vres, ares), writes=(f"bank{bnk}",), tag=f"oi{g}_{h_}")
                if g > 0:
                    em.emit("pe", (lambda h_=h_, j=j, bh=bh, mq=mq, bnk=bnk:
                            lambda h: h.matmul(
                                psf[bnk][bh:bh + 64, 128:256],
                                s_bf[bh:bh + 64, (h_ // 2) * 64:(h_ // 2) * 64 + 64],
                                fm_kq[mq][bh:bh + 64, j * 128:(j + 1) * 128],
                                start=False, stop=True))(),
                            reads=("s_bf", f"fm_kq{mq}"), writes=(f"bank{bnk}",),
                            tag=f"os{g}_{h_}")
                em.emit("dve", (lambda h_=h_, j=j, bh=bh, bnk=bnk:
                        lambda h: h.tensor_copy(
                            fm_am[h_ // 2][bh:bh + 64, j * 128:(j + 1) * 128],
                            psf[bnk][bh:bh + 64, 128:256]))(),
                        reads=(f"bank{bnk}",), writes=(f"fm_am{h_ // 2}",),
                        tag=f"oe{g}_{h_}")
                em.emit("pe", (lambda h_=h_, bh=bh, ktm=ktm, vtm=vtm:
                        lambda h: h.matmul(
                    psf[5][bh:bh + 64, (h_ // 2) * 64:(h_ // 2) * 64 + 64],
                    ktm[:, h_ * 64:(h_ + 1) * 64], vtm[:, h_ * 64:(h_ + 1) * 64],
                    start=True, stop=True))(),
                    reads=(kres, vres), writes=("bank5",), tag=f"su{g}_{h_}")
            if g < NCHUNK - 1:
                em.emit("dve", lambda h: h.tensor_add(s_f32[:], s_f32[:], psf[5][:, :]),
                        reads=("bank5", "s_f32"), writes=("s_f32",), tag=f"sa{g}")
                em.emit("dve", lambda h: h.tensor_copy(s_bf[:], s_f32[:]),
                        reads=("s_f32",), writes=("s_bf",), tag=f"sb{g}")
            for nh in range(2):
                for k in range(8):
                    em.emit("pe", (lambda k=k, nh=nh, j=j: lambda h: h.matmul(
                        psf[nh][:, :], fm_am[k][:, j * 128:(j + 1) * 128],
                        W2b[k][:, nh * 512:(nh + 1) * 512],
                        start=(k == 0), stop=(k == 7)))(),
                        reads=(f"fm_am{k}", f"W2b{k}"), writes=(f"bank{nh}",),
                        tag=f"w2_{g}_{nh}_{k}")
                em.emit("dve", (lambda nh=nh: lambda h: h.tensor_add(
                    x1_tm[:, nh * 512:(nh + 1) * 512], psf[nh][:, :],
                    cst["b2bc"][:, nh * 512:(nh + 1) * 512]))(),
                    reads=(f"bank{nh}", "b2bc"), writes=("x1_tm",), tag=f"w2e_{g}_{nh}")
            if debug:
                em.emit("act", (lambda g=g: lambda h: h.dma_start(
                    out=dbg["ktm"][g * 128:(g + 1) * 128, :], in_=ktm[:]))(),
                    reads=("ktm",), writes=("dbgout",), dma=True, tag=f"dktm{g}")
                em.emit("act", (lambda g=g: lambda h: h.dma_start(
                    out=dbg["vtm"][g * 128:(g + 1) * 128, :], in_=vtm[:]))(),
                    reads=("vtm",), writes=("dbgout",), dma=True, tag=f"dvtm{g}")
                if g % 4 == 3:
                    for k in range(8):
                        em.emit("act", (lambda G=G, k=k: lambda h: h.dma_start(
                            out=dbg["am"][G, k], in_=fm_am[k][:]))(),
                            reads=(f"fm_am{k}",), writes=("dbgout",), dma=True, tag=f"dam{G}_{k}")
            em.emit("dve", (lambda j=j, xb=xb, xres=xres: lambda h: h.tensor_add(
                x1_tm[:], x1_tm[:], xb[:, j * D:(j + 1) * D]))(),
                reads=("x1_tm", xres), writes=("x1_tm",), tag=f"res{g}")
            em.emit("sp", (lambda g=g: lambda h: h.dma_start(
                out=x1d[g * 128:(g + 1) * 128, :], in_=x1_tm[:]))(),
                reads=("x1_tm",), writes=("x1d",), dma=True, tag=f"sx1_{g}")
            ln_apply(x1_tm[:], xf32[:], "x1_tm", "xf32", EPS_LN, f"l2_{g}", par=g % 2)
            em.emit("dve", lambda h: h.tensor_copy(xf_tm[:], xf32[:]),
                    reads=("xf32",), writes=("xf_tm",), tag=f"xfc{g}")
            em.emit("act", (lambda g=g: lambda h: h.dma_start(
                out=xfd[g * 128:(g + 1) * 128, :], in_=xf_tm[:]))(),
                reads=("xf_tm",), writes=("xfd",), dma=True, tag=f"sxf_{g}")
            x2sb, x2sbr = x2s2[g % 2], f"x2s{g % 2}"
            em.emit("dve", (lambda x2sb=x2sb: lambda h: h.tensor_mul(
                x2sb[:], xf_tm[:], xf_tm[:]))(),
                reads=("xf_tm",), writes=(x2sbr,), tag=f"rq{g}")
            em.emit("dve", (lambda x2sb=x2sb: lambda h: h.reduce_sum(
                srow[:, 0:1], x2sb[:], axis=AX.X))(),
                reads=(x2sbr,), writes=("srow",), tag=f"rr{g}")
            em.emit("dve", lambda h: h.tensor_scalar(srow[:, 0:1], srow[:, 0:1],
                                                     1.0 / D, EPS_RMS, ALU.mult, ALU.add),
                    reads=("srow",), writes=("srow",), tag=f"rv{g}")
            em.emit("act", lambda h: h.sqrt(srow[:, 1:2], srow[:, 0:1]),
                    reads=("srow",), writes=("srow",), tag=f"rs{g}")
            em.emit("dve", lambda h: h.reciprocal(srow[:, 1:2], srow[:, 1:2]),
                    reads=("srow",), writes=("srow",), tag=f"rc{g}")
            em.emit("dve", lambda h: h.tensor_mul(srow[:, 2:3], srow[:, 1:2],
                                                  srow[:, 1:2]),
                    reads=("srow",), writes=("srow",), tag=f"rn1{g}")
            em.emit("dve", lambda h: h.tensor_mul(srow[:, 2:3], srow[:, 2:3],
                                                  srow[:, 0:1]),
                    reads=("srow",), writes=("srow",), tag=f"rn2{g}")
            em.emit("dve", lambda h: h.tensor_scalar(srow[:, 2:3], srow[:, 2:3],
                                                     -0.5, 1.5, ALU.mult, ALU.add),
                    reads=("srow",), writes=("srow",), tag=f"rn3{g}")
            em.emit("dve", lambda h: h.tensor_mul(srow[:, 1:2], srow[:, 1:2],
                                                  srow[:, 2:3]),
                    reads=("srow",), writes=("srow",), tag=f"rn4{g}")
            for k in range(8):
                tb = k % 2
                em.emit("pe", (lambda k=k, tb=tb: lambda h: h.transpose(
                    psf[tb][:, 128:256], xf32[:, k * 128:(k + 1) * 128],
                    cst["identf"][:]))(),
                    reads=("xf32", "identf"), writes=(f"bank{tb}",), tag=f"tf{g}_{k}")
                em.emit("dve", (lambda k=k, tb=tb: lambda h: h.tensor_copy(
                    fm_xfc[k][:], psf[tb][:, 128:256]))(),
                    reads=(f"bank{tb}",), writes=(f"fm_xfc{k}",), tag=f"tfe{g}_{k}")
            for k in range(8):
                em.emit("pe", (lambda k=k: lambda h: h.matmul(
                    psf[3][:, 0:8], fm_xfc[k][:], rwb[:, k * 8:(k + 1) * 8],
                    start=(k == 0), stop=(k == 7)))(),
                    reads=(f"fm_xfc{k}", "rwb"), writes=("bank3",), tag=f"lg{g}_{k}")
            em.emit("dve", (lambda g=g: lambda h: h.tensor_scalar_mul(
                lg_sb[:, g * 8:(g + 1) * 8], psf[3][:, 0:8], srow[:, 1:2]))(),
                reads=("bank3", "srow"), writes=("lg_sb",), tag=f"lge{g}")

    m1L, m2L = sc2[:, 0:32], sc3[:, 0:32]
    em.emit("dve", lambda h: h.tensor_copy(m1L, lg_sb[:, 0::8]),
            reads=("lg_sb",), writes=("sc2",), tag="Lm0")
    for e_ in range(1, 8):
        em.emit("dve", (lambda e_=e_: lambda h: h.tensor_max(
            m1L, m1L, lg_sb[:, e_::8]))(),
            reads=("lg_sb", "sc2"), writes=("sc2",), tag=f"Lm{e_}")
    for e_ in range(8):
        em.emit("dve", (lambda e_=e_: lambda h: h.tensor_tensor(
            eq1L[:, e_::8], lg_sb[:, e_::8], m1L, ALU.is_equal))(),
            reads=("lg_sb", "sc2"), writes=("eq1L",), tag=f"Le1{e_}")
    for e_ in range(8):
        em.emit("dve", (lambda e_=e_: lambda h: h.scalar_tensor_tensor(
            sc1[:, e_::8], eq1L[:, e_::8], -BIG, lg_sb[:, e_::8],
            ALU.mult, ALU.add))(),
            reads=("eq1L", "lg_sb"), writes=("sc1",), tag=f"Lk{e_}")
    em.emit("dve", lambda h: h.tensor_copy(m2L, sc1[:, 0::8]),
            reads=("sc1",), writes=("sc3",), tag="Ln0")
    for e_ in range(1, 8):
        em.emit("dve", (lambda e_=e_: lambda h: h.tensor_max(
            m2L, m2L, sc1[:, e_::8]))(),
            reads=("sc1", "sc3"), writes=("sc3",), tag=f"Ln{e_}")
    for e_ in range(8):
        em.emit("dve", (lambda e_=e_: lambda h: h.tensor_tensor(
            eq2L[:, e_::8], lg_sb[:, e_::8], m2L, ALU.is_equal))(),
            reads=("lg_sb", "sc3"), writes=("eq2L",), tag=f"Le2{e_}")
    em.emit("dve", lambda h: h.tensor_add(sc1[:], eq1L[:], eq2L[:]),
            reads=("eq1L", "eq2L"), writes=("sc1",), tag="Lsel")
    em.emit("pe", lambda h: h.matmul(psf[0][:, 0:256], cst["ltstrict"][:], sc1[:],
                                     start=True, stop=True),
            reads=("ltstrict", "sc1"), writes=("bank0",), tag="Lcs")
    em.emit("pe", lambda h: h.matmul(psf[1][0:1, 0:256], cst["ones_col"][:], sc1[:],
                                     start=True, stop=True),
            reads=("ones_col", "sc1"), writes=("bank1",), tag="Ltot")
    em.emit("dve", lambda h: h.tensor_copy(sc2[0:1, 0:256], psf[1][0:1, 0:256]),
            reads=("bank1",), writes=("sc2",), tag="Ltote")
    em.emit("dve", lambda h: h.tensor_copy(srow[0:1, 8:16], sc2[0:1, 0:8]),
            reads=("sc2",), writes=("srow",), tag="Lc0")
    for jj in range(1, 32):
        em.emit("dve", (lambda jj=jj: lambda h: h.tensor_add(
            srow[0:1, 8:16], srow[0:1, 8:16], sc2[0:1, jj * 8:(jj + 1) * 8]))(),
            reads=("srow", "sc2"), writes=("srow",), tag=f"Lc{jj}")
    for piece in range(2):
        em.emit("pe", (lambda piece=piece: lambda h: h.transpose(
            psf[2][:, 256 + piece:257 + piece],
            sc2[0:1, piece * 128:(piece + 1) * 128], cst["identf"][0:1, 0:1]))(),
            reads=("sc2", "identf"), writes=("bank2",), tag=f"Ltt{piece}")
    em.emit("dve", lambda h: h.tensor_copy(sc3[:, 0:2], psf[2][:, 256:258]),
            reads=("bank2",), writes=("sc3",), tag="Ltte")
    em.emit("pe", lambda h: h.matmul(psf[2][:, 0:2], cst["lt8"][:], sc3[:, 0:2],
                                     start=True, stop=True),
            reads=("lt8", "sc3"), writes=("bank2",), tag="Lo1")
    em.emit("pe", lambda h: h.matmul(psf[2][:, 2:3], cst["cs8"][:], sc3[:, 0:1],
                                     start=True, stop=True),
            reads=("cs8", "sc3"), writes=("bank2",), tag="Lo2")
    em.emit("dve", lambda h: h.tensor_copy(sc4[:, 0:2], psf[2][:, 0:2]),
            reads=("bank2",), writes=("sc4",), tag="Loe")
    em.emit("dve", lambda h: h.tensor_add(sc4[:, 1:2], sc4[:, 1:2], psf[2][:, 2:3]),
            reads=("bank2", "sc4"), writes=("sc4",), tag="Loa")
    for piece in range(2):
        em.emit("pe", (lambda piece=piece: lambda h: h.transpose(
            psf[2][0:1, 128 + piece * 128:256 + piece * 128],
            sc4[:, piece:piece + 1], cst["identf"][:]))(),
            reads=("sc4", "identf"), writes=("bank2",), tag=f"Lb{piece}")
    em.emit("dve", lambda h: h.tensor_copy(sc2[0:1, 0:256], psf[2][0:1, 128:384]),
            reads=("bank2",), writes=("sc2",), tag="Lbe")
    em.emit("pe", lambda h: h.matmul(psf[3][:, 0:256], cst["ones_row"][:],
                                     sc2[0:1, 0:256], start=True, stop=True),
            reads=("ones_row", "sc2"), writes=("bank3",), tag="Lbc")
    em.emit("dve", lambda h: h.tensor_copy(rnkL[:], psf[0][:, 0:256]),
            reads=("bank0",), writes=("rnkL",), tag="Lr0")
    em.emit("dve", lambda h: h.tensor_add(rnkL[:], rnkL[:], psf[3][:, 0:256]),
            reads=("bank3", "rnkL"), writes=("rnkL",), tag="Lr1")
    em.emit("sp", lambda h: h.dma_start(out=lgd[0:128, :], in_=lg_sb[:]),
            reads=("lg_sb",), writes=("lgd",), dma=True, tag="slg")
    em.emit("sp", lambda h: h.dma_start(out=lgd[128:129, 0:8], in_=srow[0:1, 8:16]),
            reads=("srow",), writes=("lgd",), dma=True, tag="scnt")

    em.emit("pool", lambda h: h.collective_compute(
        "AllGather", ALU.bypass, replica_groups=[list(range(8))],
        ins=[lgd.opt()], outs=[lg_all.opt()]),
        reads=("lgd",), writes=("lg_all",), cc=True, tag="AG1a")
    em.emit("pool", lambda h: h.collective_compute(
        "AllGather", ALU.bypass, replica_groups=[list(range(8))],
        ins=[xfd.opt()], outs=[xf_all.opt()]),
        reads=("xfd",), writes=("xf_all",), cc=True, tag="AG1b")

    if debug:
        em.emit("act", lambda h: h.dma_start(out=dbg["x1"], in_=x1d),
                reads=("x1d",), writes=("dbgout",), dma=True, tag="dbgx1")
        em.emit("act", lambda h: h.dma_start(out=dbg["xf"], in_=xfd),
                reads=("xfd",), writes=("dbgout",), dma=True, tag="dbgxf")
        em.emit("act", lambda h: h.dma_start(out=dbg["lg"], in_=lgd),
                reads=("lgd",), writes=("dbgout",), dma=True, tag="dbglg")

    sb_free(*(f"W1b{k}" for k in range(8)), *(f"W2b{k}" for k in range(8)),
            "x_tm0", "x_tm1", "xln_tm", *(f"fm_xln{k}" for k in range(8)),
            *(f"fm_kq{m}" for m in range(16)), "ktm0", "ktm1", "vtm0", "vtm1",
            "a_sb0", "a_sb1", "s_f32", "s_bf", *(f"fm_am{k}" for k in range(8)),
            "x1_tm", "xf_tm", "xf32", *(f"fm_xfc{k}" for k in range(8)),
            "stat0", "stat1", "x2s0", "x2s1")
    em.barrier()

    lgt = sb("lgt", [128, 2048], F32)
    gsc = sb("gsc", [128, 2048], F32)
    geq1 = sb("geq1", [128, 2048], F32)
    geq2 = sb("geq2", [128, 2048], F32)
    selc = sb("selc", [128, 256], F32)
    wdc = sb("wdc", [128, 256], F32)
    rnkc = sb("rnkc", [128, 256], F32)
    gm1 = sb("gm1", [128, 256], F32)
    gm2 = sb("gm2", [128, 256], F32)
    cnt8 = sb("cnt8", [8, 8], F32)
    base8 = sb("base8", [128, 8], F32)
    agb = [sb(f"ag{i}", [128, 128], F32) for i in range(4)]
    bgb = [sb(f"bg{i}", [128, 2 * NTC], F32) for i in range(4)]

    em.emit("sp", lambda h: h.dma_start(
        out=lgt[:].rearrange("q (r n) -> q r n", n=256),
        in_=lg_all.rearrange("(r q) n -> q r n", q=129)[0:128, :, :]),
        reads=("lg_all",), writes=("lgt",), dma=True, tag="ldlgt")
    em.emit("sp", lambda h: h.dma_start(
        out=cnt8[:, :], in_=lg_all.rearrange("(r q) n -> r q n", q=129)[:, 128, 0:8]),
        reads=("lg_all",), writes=("cnt8",), dma=True, tag="ldcnt")

    em.emit("dve", lambda h: h.tensor_copy(gm1[:], lgt[:, 0::8]),
            reads=("lgt",), writes=("gm1",), tag="Gm0")
    for e_ in range(1, 8):
        em.emit("dve", (lambda e_=e_: lambda h: h.tensor_max(
            gm1[:], gm1[:], lgt[:, e_::8]))(),
            reads=("lgt", "gm1"), writes=("gm1",), tag=f"Gm{e_}")
    for e_ in range(8):
        em.emit("dve", (lambda e_=e_: lambda h: h.tensor_tensor(
            geq1[:, e_::8], lgt[:, e_::8], gm1[:], ALU.is_equal))(),
            reads=("lgt", "gm1"), writes=("geq1",), tag=f"Ge1{e_}")
    for e_ in range(8):
        em.emit("dve", (lambda e_=e_: lambda h: h.scalar_tensor_tensor(
            gsc[:, e_::8], geq1[:, e_::8], -BIG, lgt[:, e_::8], ALU.mult, ALU.add))(),
            reads=("geq1", "lgt"), writes=("gsc",), tag=f"Gk{e_}")
    em.emit("dve", lambda h: h.tensor_copy(gm2[:], gsc[:, 0::8]),
            reads=("gsc",), writes=("gm2",), tag="Gn0")
    for e_ in range(1, 8):
        em.emit("dve", (lambda e_=e_: lambda h: h.tensor_max(
            gm2[:], gm2[:], gsc[:, e_::8]))(),
            reads=("gsc", "gm2"), writes=("gm2",), tag=f"Gn{e_}")
    for e_ in range(8):
        em.emit("dve", (lambda e_=e_: lambda h: h.tensor_tensor(
            geq2[:, e_::8], lgt[:, e_::8], gm2[:], ALU.is_equal))(),
            reads=("lgt", "gm2"), writes=("geq2",), tag=f"Ge2{e_}")
    em.emit("dve", lambda h: h.tensor_sub(sc1[:], gm1[:], gm2[:]),
            reads=("gm1", "gm2"), writes=("sc1",), tag="Gd")
    em.emit("act", lambda h: h.activation(sc1[:], sc1[:], AF.Sigmoid),
            reads=("sc1",), writes=("sc1",), tag="Gsig")
    em.emit("dve", lambda h: h.tensor_scalar(sc2[:], sc1[:], -1.0, 1.0,
                                             ALU.mult, ALU.add),
            reads=("sc1",), writes=("sc2",), tag="Grw2")
    em.emit("dve", lambda h: h.memset(selc[:], 0.0), writes=("selc",), tag="Gs0")
    em.emit("dve", lambda h: h.memset(wdc[:], 0.0), writes=("wdc",), tag="Gw0")
    for e_ in range(8):
        em.emit("dve", (lambda e_=e_: lambda h: h.scalar_tensor_tensor(
            sc3[:], geq1[:, e_::8], 1.0, geq2[:, e_::8], ALU.mult, ALU.add))(),
            reads=("geq1", "geq2"), writes=("sc3",), tag=f"Gse{e_}")
        em.emit("dve", (lambda e_=e_: lambda h: h.scalar_tensor_tensor(
            selc[:], sc3[:], cst["csel"][:, e_:e_ + 1], selc[:], ALU.mult, ALU.add))(),
            reads=("sc3", "csel", "selc"), writes=("selc",), tag=f"Gsa{e_}")
        em.emit("dve", (lambda e_=e_: lambda h: h.tensor_mul(
            sc3[:], geq1[:, e_::8], sc1[:]))(),
            reads=("geq1", "sc1"), writes=("sc3",), tag=f"Gw1{e_}")
        em.emit("dve", (lambda e_=e_: lambda h: h.tensor_mul(
            sc4[:], geq2[:, e_::8], sc2[:]))(),
            reads=("geq2", "sc2"), writes=("sc4",), tag=f"Gw3{e_}")
        em.emit("dve", lambda h: h.tensor_add(sc3[:], sc3[:], sc4[:]),
                reads=("sc3", "sc4"), writes=("sc3",), tag=f"Gw4{e_}")
        em.emit("dve", (lambda e_=e_: lambda h: h.scalar_tensor_tensor(
            wdc[:], sc3[:], cst["csel"][:, e_:e_ + 1], wdc[:], ALU.mult, ALU.add))(),
            reads=("sc3", "csel", "wdc"), writes=("wdc",), tag=f"Gw5{e_}")
    em.emit("pe", lambda h: h.matmul(psf[0][:, 0:256], cst["ltstrict"][:], selc[:],
                                     start=True, stop=True),
            reads=("ltstrict", "selc"), writes=("bank0",), tag="Gcs")
    em.emit("pe", lambda h: h.matmul(psf[1][0:1, 0:256], cst["ones_col"][:], selc[:],
                                     start=True, stop=True),
            reads=("ones_col", "selc"), writes=("bank1",), tag="Gtot")
    em.emit("dve", lambda h: h.tensor_copy(sc2[0:1, 0:256], psf[1][0:1, 0:256]),
            reads=("bank1",), writes=("sc2",), tag="Gtote")
    for piece in range(2):
        em.emit("pe", (lambda piece=piece: lambda h: h.transpose(
            psf[2][:, 256 + piece:257 + piece],
            sc2[0:1, piece * 128:(piece + 1) * 128], cst["identf"][0:1, 0:1]))(),
            reads=("sc2", "identf"), writes=("bank2",), tag=f"Gtt{piece}")
    em.emit("dve", lambda h: h.tensor_copy(sc3[:, 0:2], psf[2][:, 256:258]),
            reads=("bank2",), writes=("sc3",), tag="Gtte")
    em.emit("pe", lambda h: h.matmul(psf[2][:, 0:2], cst["ltstrict"][:], sc3[:, 0:2],
                                     start=True, stop=True),
            reads=("ltstrict", "sc3"), writes=("bank2",), tag="Go1")
    em.emit("pe", lambda h: h.matmul(psf[4][0:1, 0:1], cst["ones_col"][:], sc3[:, 0:1],
                                     start=True, stop=True),
            reads=("ones_col", "sc3"), writes=("bank4",), tag="Go2")
    em.emit("dve", lambda h: h.tensor_copy(srow[0:1, 2:3], psf[4][0:1, 0:1]),
            reads=("bank4",), writes=("srow",), tag="Go2e")
    em.emit("pe", lambda h: h.matmul(psf[4][:, 0:1], cst["ones_row"][:],
                                     srow[0:1, 2:3], start=True, stop=True),
            reads=("ones_row", "srow"), writes=("bank4",), tag="Go2b")
    em.emit("dve", lambda h: h.tensor_copy(sc4[:, 0:2], psf[2][:, 0:2]),
            reads=("bank2",), writes=("sc4",), tag="Goe")
    em.emit("dve", lambda h: h.tensor_add(sc4[:, 1:2], sc4[:, 1:2], psf[4][:, 0:1]),
            reads=("bank4", "sc4"), writes=("sc4",), tag="Goa")
    for piece in range(2):
        em.emit("pe", (lambda piece=piece: lambda h: h.transpose(
            psf[2][0:1, 128 + piece * 128:256 + piece * 128],
            sc4[:, piece:piece + 1], cst["identf"][:]))(),
            reads=("sc4", "identf"), writes=("bank2",), tag=f"Gb{piece}")
    em.emit("dve", lambda h: h.tensor_copy(sc2[0:1, 0:256], psf[2][0:1, 128:384]),
            reads=("bank2",), writes=("sc2",), tag="Gbe")
    em.emit("pe", lambda h: h.matmul(psf[3][:, 0:256], cst["ones_row"][:],
                                     sc2[0:1, 0:256], start=True, stop=True),
            reads=("ones_row", "sc2"), writes=("bank3",), tag="Gbc")
    em.emit("dve", lambda h: h.tensor_copy(rnkc[:], psf[0][:, 0:256]),
            reads=("bank0",), writes=("rnkc",), tag="Gr0")
    em.emit("dve", lambda h: h.tensor_add(rnkc[:], rnkc[:], psf[3][:, 0:256]),
            reads=("bank3", "rnkc"), writes=("rnkc",), tag="Gr1")
    em.emit("pe", lambda h: h.matmul(psf[4][0:1, 0:8], cst["prevmask"][:], cnt8[:, :],
                                     start=True, stop=True),
            reads=("prevmask", "cnt8"), writes=("bank4",), tag="Gba")
    em.emit("dve", lambda h: h.tensor_copy(srow[0:1, 8:16], psf[4][0:1, 0:8]),
            reads=("bank4",), writes=("srow",), tag="Gbb")
    em.emit("pe", lambda h: h.matmul(psf[4][:, 0:8], cst["ones_row"][:],
                                     srow[0:1, 8:16], start=True, stop=True),
            reads=("ones_row", "srow"), writes=("bank4",), tag="Gbc2")
    em.emit("dve", lambda h: h.tensor_copy(base8[:], psf[4][:, 0:8]),
            reads=("bank4",), writes=("base8",), tag="Gbd")
    for jj in range(NCHUNK):
        em.emit("dve", (lambda jj=jj: lambda h: h.tensor_add(
            rnkL[:, jj * 8:(jj + 1) * 8], rnkL[:, jj * 8:(jj + 1) * 8], base8[:]))(),
            reads=("rnkL", "base8"), writes=("rnkL",), tag=f"Grl{jj}")

    em.emit("dve", lambda h: h.tensor_single_scalar(sc1[:], rnkc[:], float(CAP),
                                                    ALU.is_lt),
            reads=("rnkc",), writes=("sc1",), tag="Dk")
    em.emit("dve", lambda h: h.tensor_mul(sc1[:], sc1[:], selc[:]),
            reads=("sc1", "selc"), writes=("sc1",), tag="Dk2")
    em.emit("dve", lambda h: h.tensor_mul(sc2[:], rnkc[:], sc1[:]),
            reads=("rnkc", "sc1"), writes=("sc2",), tag="Dm1")
    em.emit("dve", lambda h: h.tensor_scalar(sc3[:], sc1[:], -BIG, BIG,
                                             ALU.mult, ALU.add),
            reads=("sc1",), writes=("sc3",), tag="Dm1b")
    em.emit("dve", lambda h: h.tensor_add(sc2[:], sc2[:], sc3[:]),
            reads=("sc2", "sc3"), writes=("sc2",), tag="Dm2")
    idiv = sb("idiv", [128, 256], I32)
    em.emit("dve", lambda h: h.tensor_scalar(sc3[:], sc2[:], 1.0 / 128.0, -0.4999,
                                             ALU.mult, ALU.add),
            reads=("sc2",), writes=("sc3",), tag="Dd1")
    em.emit("dve", lambda h: h.tensor_copy(idiv[:], sc3[:]),
            reads=("sc3",), writes=("idiv",), tag="Dd2")
    em.emit("dve", lambda h: h.tensor_copy(sc3[:], idiv[:]),
            reads=("idiv",), writes=("sc3",), tag="Dd3")
    em.emit("dve", lambda h: h.scalar_tensor_tensor(sc4[:], sc3[:], -128.0, sc2[:],
                                                    ALU.mult, ALU.add),
            reads=("sc3", "sc2"), writes=("sc4",), tag="Dd4")
    for g in range(256):
        ag, bg = agb[g % 4], bgb[g % 4]
        ar, br = f"agg{g % 4}", f"bgg{g % 4}"
        em.emit("dve", (lambda g=g, ag=ag: lambda h: h.tensor_scalar(
            ag[:], cst["iota_f128"][:], sc4[:, g:g + 1], None, ALU.is_equal))(),
            reads=("sc4", "iota_f128"), writes=(ar,), tag=f"Da{g}")
        em.emit("dve", (lambda g=g, bg=bg: lambda h: h.tensor_scalar(
            bg[:, 0:NTC], cst["iota_f40"][:], sc3[:, g:g + 1], None, ALU.is_equal))(),
            reads=("sc3", "iota_f40"), writes=(br,), tag=f"Db{g}")
        em.emit("dve", (lambda g=g, bg=bg: lambda h: h.tensor_scalar_mul(
            bg[:, NTC:2 * NTC], bg[:, 0:NTC], cst["tok_pg"][:, g:g + 1]))(),
            reads=(br, "tok_pg"), writes=(br,), tag=f"Dbt{g}")
        em.emit("dve", (lambda g=g, bg=bg: lambda h: h.tensor_scalar_mul(
            bg[:, 0:NTC], bg[:, 0:NTC], wdc[:, g:g + 1]))(),
            reads=(br, "wdc"), writes=(br,), tag=f"Dbw{g}")
        em.emit("pe", (lambda g=g, ag=ag, bg=bg: lambda h: h.matmul(
            psf[5][:, 0:NTC], ag[:], bg[:, NTC:2 * NTC],
            start=(g == 0), stop=(g == 255)))(),
            reads=(ar, br), writes=("bank5",), tag=f"Dp{g}")
        em.emit("pe", (lambda g=g, ag=ag, bg=bg: lambda h: h.matmul(
            psf[6][:, 0:NTC], ag[:], bg[:, 0:NTC],
            start=(g == 0), stop=(g == 255)))(),
            reads=(ar, br), writes=("bank6",), tag=f"Dq{g}")
    em.emit("dve", lambda h: h.tensor_copy(disp_i[:], psf[5][:, 0:NTC]),
            reads=("bank5",), writes=("disp_i",), tag="Dde")
    em.emit("dve", lambda h: h.tensor_copy(wv_sb[:], psf[6][:, 0:NTC]),
            reads=("bank6",), writes=("wv_sb",), tag="Dwe")

    if debug:
        em.emit("act", lambda h: h.dma_start(out=dbg["disp"], in_=disp_i[:]),
                reads=("disp_i",), writes=("dbgout",), dma=True, tag="dbgdisp")
        em.emit("act", lambda h: h.dma_start(out=dbg["wv"], in_=wv_sb[:]),
                reads=("wv_sb",), writes=("dbgout",), dma=True, tag="dbgwv")
        em.emit("act", lambda h: h.dma_start(out=dbg["rnk"], in_=rnkc[:]),
                reads=("rnkc",), writes=("dbgout",), dma=True, tag="dbgrnk")
        em.emit("act", lambda h: h.dma_start(out=dbg["sel"], in_=selc[:]),
                reads=("selc",), writes=("dbgout",), dma=True, tag="dbgsel")
        em.emit("act", lambda h: h.dma_start(out=dbg["wdc"], in_=wdc[:]),
                reads=("wdc",), writes=("dbgout",), dma=True, tag="dbgwdc")

    sb_free("lgt", "gsc", "geq1", "geq2", "gm1", "gm2", "cnt8", "base8",
            "ag0", "ag1", "ag2", "ag3", "bg0", "bg1", "bg2", "bg3",
            "idiv", "selc", "wdc", "rnkc")
    em.barrier()

    NGB = 10
    xg_tm = [sb(f"xg_tm{i}", [128, D], BF16) for i in range(2)]
    fm_xg = [sb(f"fm_xg{k}", [128, 512], BF16) for k in range(8)]
    guB = sb("guB", [128, 32 * 512], BF16)
    rgw = [sb(f"rgw{i}", [128, 1024], BF16) for i in range(3)]
    ruw = [sb(f"ruw{i}", [128, 1024], BF16) for i in range(3)]
    rdw = [sb(f"rdw{i}", [128, 1024], BF16) for i in range(3)]
    gsil = [sb(f"gsil{i}", [128, 512], BF16) for i in range(2)]
    ct_bf4 = [sb(f"ct_bf{i}", [128, D], BF16) for i in range(4)]

    for grp in range(NGB):
        for t4 in range(4):
            tile = grp * 4 + t4
            xr = f"xg{t4 % 2}"
            em.emit("pool", (lambda tile=tile, t4=t4: lambda h: h.indirect_dma_start(
                out=xg_tm[t4 % 2][:], out_offset=None, in_=xf_all,
                in_offset=bass.IndirectOffsetOnAxis(disp_i[:, tile:tile + 1], axis=0),
                bounds_check=regs["xg"], oob_is_err=False))(),
                reads=("xf_all", "disp_i"), writes=(xr,), dma=True, tag=f"xg{tile}")
            for k in range(8):
                em.emit("pe", (lambda k=k, t4=t4: lambda h: h.transpose(
                    psb[:, 0:128], xg_tm[t4 % 2][:, k * 128:(k + 1) * 128],
                    cst["identb"][:]))(),
                    reads=(xr, "identb"), writes=(BKB,), tag=f"xgt{tile}_{k}")
                em.emit("dve", (lambda k=k, t4=t4: lambda h: h.tensor_copy(
                    fm_xg[k][:, t4 * 128:(t4 + 1) * 128], psb[:, 0:128]))(),
                    reads=(BKB,), writes=(f"fm_xg{k}",), tag=f"xgte{tile}_{k}")
        for i in range(32):
            wb, wres = rgw[i % 3], f"rgw{i % 3}"
            ub, ures = ruw[i % 3], f"ruw{i % 3}"
            em.emit("sp", (lambda i=i, wb=wb: lambda h: h.dma_start(
                out=wb[:], in_=rgb[i].rearrange("p dk f -> p (dk f)")))(),
                reads=("rgb",), writes=(wres,), dma=True, tag=f"lrg{grp}_{i}")
            em.emit("act", (lambda i=i, ub=ub: lambda h: h.dma_start(
                out=ub[:], in_=rub[i].rearrange("p dk f -> p (dk f)")))(),
                reads=("rub",), writes=(ures,), dma=True, tag=f"lru{grp}_{i}")
            bg_, bu_ = ((0, 1), (2, 3), (4, 5))[i % 3]
            for k in range(8):
                em.emit("pe", (lambda k=k, wb=wb, bg_=bg_, wres=wres: lambda h: h.matmul(
                    psf[bg_][:, :], wb[:, k * 128:(k + 1) * 128], fm_xg[k][:],
                    start=(k == 0), stop=(k == 7)))(),
                    reads=(wres, f"fm_xg{k}"), writes=(f"bank{bg_}",),
                    tag=f"eg{grp}_{i}_{k}")
            for k in range(8):
                em.emit("pe", (lambda k=k, ub=ub, bu_=bu_, ures=ures: lambda h: h.matmul(
                    psf[bu_][:, :], ub[:, k * 128:(k + 1) * 128], fm_xg[k][:],
                    start=(k == 0), stop=(k == 7)))(),
                    reads=(ures, f"fm_xg{k}"), writes=(f"bank{bu_}",),
                    tag=f"eu{grp}_{i}_{k}")
            gsl, gslr = gsil[i % 2], f"gsil{i % 2}"
            em.emit("act", (lambda bg_=bg_, gsl=gsl: lambda h: h.activation(
                gsl[:], psf[bg_][:, :], AF.Silu))(),
                reads=(f"bank{bg_}",), writes=(gslr,), tag=f"es{grp}_{i}")
            em.emit("dve", (lambda i=i, bu_=bu_, gsl=gsl: lambda h: h.tensor_mul(
                guB[:, i * 512:(i + 1) * 512], psf[bu_][:, :], gsl[:]))(),
                reads=(f"bank{bu_}", gslr), writes=("guB",), tag=f"em{grp}_{i}")
        for nh in range(2):
            for i in range(32):
                db, dres = rdw[i % 3], f"rdw{i % 3}"
                em.emit(("sp", "act")[i % 2], (lambda i=i, db=db, nh=nh:
                        lambda h: h.dma_start(
                    out=db[:, 0:512], in_=rdb[i * 128:(i + 1) * 128,
                                             nh * 512:(nh + 1) * 512]))(),
                    reads=("rdb",), writes=(dres,), dma=True, tag=f"lrd{grp}_{nh}_{i}")
                for t4 in range(4):
                    em.emit("pe", (lambda i=i, t4=t4, db=db, dres=dres:
                            lambda h: h.matmul(
                                psf[2 + t4][:, :],
                                guB[:, i * 512 + t4 * 128:i * 512 + (t4 + 1) * 128],
                                db[:, 0:512],
                                start=(i == 0), stop=(i == 31)))(),
                            reads=("guB", dres), writes=(f"bank{2 + t4}",),
                            tag=f"ed{grp}_{nh}_{t4}_{i}")
            for t4 in range(4):
                tile = grp * 4 + t4
                em.emit("dve", (lambda tile=tile, nh=nh, t4=t4:
                        lambda h: h.tensor_scalar_mul(
                            ct_bf4[t4][:, nh * 512:(nh + 1) * 512],
                            psf[2 + t4][:, :], wv_sb[:, tile:tile + 1]))(),
                        reads=(f"bank{2 + t4}", "wv_sb"), writes=(f"ct_bf{t4}",),
                        tag=f"ece{grp}_{nh}_{t4}")
        for t4 in range(4):
            tile = grp * 4 + t4
            em.emit("sp", (lambda tile=tile, t4=t4: lambda h: h.dma_start(
                out=ctd[tile * 128:(tile + 1) * 128, :], in_=ct_bf4[t4][:]))(),
                reads=(f"ct_bf{t4}",), writes=("ctd",), dma=True, tag=f"sct{tile}")

    for which, eqX, eres in ((0, eq1L, "eq1L"), (1, eq2L, "eq2L")):
        em.emit("dve", lambda h: h.memset(sc1[:, 0:32], 0.0), writes=("sc1",),
                tag=f"Re0_{which}")
        em.emit("dve", lambda h: h.memset(sc2[:, 0:32], 0.0), writes=("sc2",),
                tag=f"Rr0_{which}")
        for e_ in range(8):
            if e_ > 0:
                em.emit("dve", (lambda e_=e_, eqX=eqX: lambda h: h.scalar_tensor_tensor(
                    sc1[:, 0:32], eqX[:, e_::8], float(e_), sc1[:, 0:32],
                    ALU.mult, ALU.add))(),
                    reads=(eres, "sc1"), writes=("sc1",), tag=f"Re{which}_{e_}")
            em.emit("dve", (lambda e_=e_, eqX=eqX: lambda h: h.tensor_mul(
                sc3[:, 0:32], eqX[:, e_::8], rnkL[:, e_::8]))(),
                reads=(eres, "rnkL"), writes=("sc3",), tag=f"Rm{which}_{e_}")
            em.emit("dve", lambda h: h.tensor_add(sc2[:, 0:32], sc2[:, 0:32],
                                                  sc3[:, 0:32]),
                    reads=("sc2", "sc3"), writes=("sc2",), tag=f"Ra{which}_{e_}")
        em.emit("dve", lambda h: h.tensor_single_scalar(
            sc3[:, 0:32], sc2[:, 0:32], float(CAP), ALU.is_lt),
            reads=("sc2",), writes=("sc3",), tag=f"Rk{which}")
        em.emit("dve", lambda h: h.scalar_tensor_tensor(
            sc4[:, 0:32], sc1[:, 0:32], float(CAP), sc2[:, 0:32], ALU.mult, ALU.add),
            reads=("sc1", "sc2"), writes=("sc4",), tag=f"Rc{which}")
        em.emit("dve", lambda h: h.tensor_mul(
            sc4[:, 0:32], sc4[:, 0:32], sc3[:, 0:32]),
            reads=("sc4", "sc3"), writes=("sc4",), tag=f"Rb{which}")
        em.emit("dve", lambda h: h.tensor_scalar(
            sc2[:, 0:32], sc3[:, 0:32], -BIG, BIG, ALU.mult, ALU.add),
            reads=("sc3",), writes=("sc2",), tag=f"Rf{which}a")
        em.emit("dve", lambda h: h.tensor_add(
            sc4[:, 0:32], sc4[:, 0:32], sc2[:, 0:32]),
            reads=("sc4", "sc2"), writes=("sc4",), tag=f"Rf{which}")
        em.emit("dve", (lambda which=which: lambda h: h.tensor_copy(
            retoff_i[:, which::2], sc4[:, 0:32]))(),
            reads=("sc4",), writes=("retoff_i",), tag=f"Ri{which}")


    if debug:
        em.emit("act", lambda h: h.dma_start(out=dbg["ro"], in_=retoff_i[:]),
                reads=("retoff_i",), writes=("dbgout",), dma=True, tag="dbgro")

    em.emit("pool", lambda h: h.collective_compute(
        "AllGather", ALU.bypass, replica_groups=[list(range(8))],
        ins=[ctd.opt()], outs=[ct_all.opt()]),
        reads=("ctd",), writes=("ct_all",), cc=True, tag="AG2")

    if debug:
        em.emit("act", lambda h: h.dma_start(out=dbg["ct"], in_=ctd),
                reads=("ctd",), writes=("dbgout",), dma=True, tag="dbgct")

    sb_free("xg_tm0", "xg_tm1", *(f"fm_xg{k}" for k in range(8)), "guB",
            "rgw0", "rgw1", "rgw2", "ruw0", "ruw1", "ruw2",
            "rdw0", "rdw1", "rdw2", "gsil0", "gsil1",
            "ct_bf0", "ct_bf1", "ct_bf2", "ct_bf3")
    em.barrier()

    xfg = sb("xfg", [128, 4 * D], F32)
    fm_xf = [sb(f"fm_xf{k}", [128, 512], BF16) for k in range(8)]
    guC = sb("guC", [128, 64 * 512], BF16)
    sgw = [sb(f"sgw{i}", [128, 1024], BF16) for i in range(3)]
    suw = [sb(f"suw{i}", [128, 1024], BF16) for i in range(3)]
    sdw = [sb(f"sdw{i}", [128, 1024], BF16) for i in range(3)]
    gsil2 = [sb(f"gsil2_{i}", [128, 512], BF16) for i in range(2)]
    sh_sb4 = [sb(f"sh_sb{i}", [128, D], F32) for i in range(4)]
    xw = [sb(f"xw{i}", [128, D], F32) for i in range(2)]
    g0 = [sb(f"g0_{i}", [128, D], BF16) for i in range(2)]
    g1 = [sb(f"g1_{i}", [128, D], BF16) for i in range(2)]
    yw = [sb(f"yw{i}", [128, D], F32) for i in range(2)]

    for G in range(NGROUP):
        em.emit("pool", (lambda G=G: lambda h: h.dma_start(
            out=xfg[:].rearrange("p (j d) -> p j d", d=D),
            in_=xfd[G * 512:(G + 1) * 512, :].rearrange(
                "(j p) d -> p j d", p=128)))(),
            reads=("xfd",), writes=("xfg",), dma=True, tag=f"lxf{G}")
        for j in range(4):
            for k in range(8):
                tb = k % 2
                em.emit("pe", (lambda k=k, j=j, tb=tb: lambda h: h.transpose(
                    psf[tb][:, 128:256], xfg[:, j * D + k * 128:j * D + (k + 1) * 128],
                    cst["identf"][:]))(),
                    reads=("xfg", "identf"), writes=(f"bank{tb}",), tag=f"xft{G}_{j}_{k}")
                em.emit("dve", (lambda k=k, j=j, tb=tb: lambda h: h.tensor_copy(
                    fm_xf[k][:, j * 128:(j + 1) * 128], psf[tb][:, 128:256]))(),
                    reads=(f"bank{tb}",), writes=(f"fm_xf{k}",), tag=f"xfte{G}_{j}_{k}")
        for i in range(64):
            wb, wres = sgw[i % 3], f"sgw{i % 3}"
            ub, ures = suw[i % 3], f"suw{i % 3}"
            em.emit("sp", (lambda i=i, wb=wb: lambda h: h.dma_start(
                out=wb[:], in_=sgb[i].rearrange("p dk f -> p (dk f)")))(),
                reads=("sgb",), writes=(wres,), dma=True, tag=f"lsg{G}_{i}")
            em.emit("act", (lambda i=i, ub=ub: lambda h: h.dma_start(
                out=ub[:], in_=sub[i].rearrange("p dk f -> p (dk f)")))(),
                reads=("sub",), writes=(ures,), dma=True, tag=f"lsu{G}_{i}")
            bg_, bu_ = ((0, 1), (2, 3), (4, 5))[i % 3]
            for k in range(8):
                em.emit("pe", (lambda k=k, wb=wb, bg_=bg_, wres=wres: lambda h: h.matmul(
                    psf[bg_][:, :], wb[:, k * 128:(k + 1) * 128], fm_xf[k][:],
                    start=(k == 0), stop=(k == 7)))(),
                    reads=(wres, f"fm_xf{k}"), writes=(f"bank{bg_}",),
                    tag=f"sg{G}_{i}_{k}")
            for k in range(8):
                em.emit("pe", (lambda k=k, ub=ub, bu_=bu_, ures=ures: lambda h: h.matmul(
                    psf[bu_][:, :], ub[:, k * 128:(k + 1) * 128], fm_xf[k][:],
                    start=(k == 0), stop=(k == 7)))(),
                    reads=(ures, f"fm_xf{k}"), writes=(f"bank{bu_}",),
                    tag=f"su{G}_{i}_{k}")
            gsl, gslr = gsil2[i % 2], f"gsil2_{i % 2}"
            em.emit("act", (lambda bg_=bg_, gsl=gsl: lambda h: h.activation(
                gsl[:], psf[bg_][:, :], AF.Silu))(),
                reads=(f"bank{bg_}",), writes=(gslr,), tag=f"ss{G}_{i}")
            em.emit("dve", (lambda i=i, bu_=bu_, gsl=gsl: lambda h: h.tensor_mul(
                guC[:, i * 512:(i + 1) * 512], psf[bu_][:, :], gsl[:]))(),
                reads=(f"bank{bu_}", gslr), writes=("guC",), tag=f"sm{G}_{i}")
        for nh in range(2):
            for i in range(64):
                db, dres = sdw[i % 3], f"sdw{i % 3}"
                em.emit(("sp", "act")[i % 2], (lambda i=i, db=db, nh=nh:
                        lambda h: h.dma_start(
                    out=db[:, 0:512], in_=sdb[i * 128:(i + 1) * 128,
                                             nh * 512:(nh + 1) * 512]))(),
                    reads=("sdb",), writes=(dres,), dma=True, tag=f"lsd{G}_{nh}_{i}")
                for t4 in range(4):
                    em.emit("pe", (lambda i=i, t4=t4, db=db, dres=dres:
                            lambda h: h.matmul(
                                psf[2 + t4][:, :],
                                guC[:, i * 512 + t4 * 128:i * 512 + (t4 + 1) * 128],
                                db[:, 0:512],
                                start=(i == 0), stop=(i == 63)))(),
                            reads=("guC", dres), writes=(f"bank{2 + t4}",),
                            tag=f"sd{G}_{nh}_{t4}_{i}")
            for t4 in range(4):
                em.emit("dve", (lambda nh=nh, t4=t4: lambda h: h.tensor_copy(
                    sh_sb4[t4][:, nh * 512:(nh + 1) * 512], psf[2 + t4][:, :]))(),
                    reads=(f"bank{2 + t4}",), writes=(f"sh_sb{t4}",),
                    tag=f"sce{G}_{nh}_{t4}")
        for t4 in range(4):
            tile = G * 4 + t4
            shb, shr = sh_sb4[t4], f"sh_sb{t4}"
            if debug:
                em.emit("sp", (lambda tile=tile, shb=shb: lambda h: h.dma_start(
                    out=shd[tile * 128:(tile + 1) * 128, :], in_=shb[:]))(),
                    reads=(shr,), writes=("shd",), dma=True, tag=f"ssh{tile}")
            p = t4 % 2
            em.emit("sp", (lambda tile=tile, p=p: lambda h: h.dma_start(
                out=xw[p][:], in_=x1d[tile * 128:(tile + 1) * 128, :]))(),
                reads=("x1d",), writes=(f"xw{p}",), dma=True, tag=f"fx{tile}")
            em.emit("dve", (lambda p=p: lambda h: h.memset(g0[p][:], 0.0))(),
                    writes=(f"g0_{p}",), tag=f"fz0{tile}")
            em.emit("dve", (lambda p=p: lambda h: h.memset(g1[p][:], 0.0))(),
                    writes=(f"g1_{p}",), tag=f"fz1{tile}")
            em.emit("pool", (lambda g=tile, p=p: lambda h: h.indirect_dma_start(
                out=g0[p][:], out_offset=None, in_=ct_all,
                in_offset=bass.IndirectOffsetOnAxis(
                    retoff_i[:, 2 * g:2 * g + 1], axis=0),
                bounds_check=regs["ct"], oob_is_err=False))(),
                reads=("ct_all", "retoff_i", f"g0_{p}"), writes=(f"g0_{p}",),
                dma=True, tag=f"fg0{tile}")
            em.emit("pool", (lambda g=tile, p=p: lambda h: h.indirect_dma_start(
                out=g1[p][:], out_offset=None, in_=ct_all,
                in_offset=bass.IndirectOffsetOnAxis(
                    retoff_i[:, 2 * g + 1:2 * g + 2], axis=0),
                bounds_check=regs["ct"], oob_is_err=False))(),
                reads=("ct_all", "retoff_i", f"g1_{p}"), writes=(f"g1_{p}",),
                dma=True, tag=f"fg1{tile}")
            em.emit("dve", (lambda p=p, shb=shb, shr=shr: lambda h: h.tensor_add(
                yw[p][:], xw[p][:], shb[:]))(),
                reads=(f"xw{p}", shr), writes=(f"yw{p}",), tag=f"fa{tile}")
            em.emit("dve", (lambda p=p: lambda h: h.tensor_add(
                yw[p][:], yw[p][:], g0[p][:]))(),
                reads=(f"yw{p}", f"g0_{p}"), writes=(f"yw{p}",), tag=f"fb{tile}")
            em.emit("dve", (lambda p=p: lambda h: h.tensor_add(
                yw[p][:], yw[p][:], g1[p][:]))(),
                reads=(f"yw{p}", f"g1_{p}"), writes=(f"yw{p}",), tag=f"fc{tile}")
            em.emit("sp", (lambda tile=tile, p=p: lambda h: h.dma_start(
                out=y_out[tile * 128:(tile + 1) * 128, :], in_=yw[p][:]))(),
                reads=(f"yw{p}",), writes=("y",), dma=True, tag=f"fy{tile}")

    if debug:
        em.emit("act", lambda h: h.dma_start(out=dbg["sh"], in_=shd),
                reads=("shd",), writes=("dbgout",), dma=True, tag="dbgsh")

    em.barrier()
    em.finalize()

    def make_block():
        with nc.Block() as block:
            @block.tensor
            def _(t):
                em.replay("pe", t)

            @block.vector
            def _(v):
                em.replay("dve", v)

            @block.scalar
            def _(s):
                em.replay("act", s)

            @block.gpsimd
            def _(g_):
                regs["xg"] = g_.to_reg(8 * T_LOC - 1)
                regs["ct"] = g_.to_reg(8 * CAP - 1)
                em.replay("pool", g_)

            @block.sync
            def _(sy):
                em.replay("sp", sy)
    make_block()
    return nc


def make_consts():
    c = {}
    c["identb"] = np.eye(128, dtype=BF16NP)
    c["identf"] = np.eye(128, dtype=np.float32)
    c["utmask"] = np.triu(np.ones((128, 128), np.float32))
    c["ltstrict"] = np.triu(np.ones((128, 128), np.float32), 1)
    pp, qq = np.meshgrid(np.arange(128), np.arange(128), indexing="ij")
    c["lt8"] = (((pp % 8) == (qq % 8)) & ((pp // 8) < (qq // 8))).astype(np.float32)
    c["cs8"] = ((pp % 8) == (qq % 8)).astype(np.float32)
    c["iota_f128"] = np.broadcast_to(np.arange(128, dtype=np.float32),
                                     (128, 128)).copy()
    c["iota_f40"] = np.broadcast_to(np.arange(NTC, dtype=np.float32),
                                    (128, NTC)).copy()
    c["iota_p"] = np.arange(128, dtype=np.float32).reshape(128, 1).copy()
    c["tok_pg"] = (np.arange(256, dtype=np.float32)[None, :] * 128
                   + np.arange(128, dtype=np.float32)[:, None]).copy()
    c["ones_col"] = np.ones((128, 1), np.float32)
    c["ones_row"] = np.ones((1, 128), np.float32)
    return c


_CACHED = {}


def kernel(L0_attn_x, L0_attn_W1, L0_attn_b1, L0_attn_W2, L0_attn_b2,
           L0_ffn_router_w, L0_ffn_rg, L0_ffn_ru, L0_ffn_rd,
           L0_ffn_sg, L0_ffn_su, L0_ffn_sd, _debug=False):
    if ("nc", _debug) not in _CACHED:
        _CACHED[("nc", _debug)] = build_program(debug=_debug)
    nc = _CACHED[("nc", _debug)]

    x = np.ascontiguousarray(np.asarray(L0_attn_x, dtype=np.float32))
    W1 = np.ascontiguousarray(np.asarray(L0_attn_W1, dtype=np.float32))
    b1 = np.asarray(L0_attn_b1, dtype=np.float32)
    W2 = np.ascontiguousarray(np.asarray(L0_attn_W2, dtype=np.float32))
    b2 = np.asarray(L0_attn_b2, dtype=np.float32)
    rw = np.asarray(L0_ffn_router_w, dtype=np.float32)
    rg = np.asarray(L0_ffn_rg, dtype=np.float32)
    ru = np.asarray(L0_ffn_ru, dtype=np.float32)
    rd = np.asarray(L0_ffn_rd, dtype=np.float32)
    sg = np.ascontiguousarray(np.asarray(L0_ffn_sg, dtype=np.float32))
    su = np.ascontiguousarray(np.asarray(L0_ffn_su, dtype=np.float32))
    sd = np.ascontiguousarray(np.asarray(L0_ffn_sd, dtype=np.float32))

    consts = make_consts()
    b1c = np.ascontiguousarray(b1.reshape(24, 128).T)
    b1kv = np.ascontiguousarray(np.broadcast_to(
        np.concatenate([b1[0:D], b1[2 * D:3 * D]]), (128, 2 * D)))
    b2bc = np.ascontiguousarray(np.broadcast_to(b2, (128, D)))
    rwT = np.ascontiguousarray(rw.T)

    in_maps = []
    for c in range(8):
        csel = np.zeros((128, E), np.float32)
        csel[:, c] = 1.0
        pm = np.zeros((E, 1), np.float32)
        pm[:c, 0] = 1.0
        m = dict(x=np.ascontiguousarray(x[c]), W1=W1, b1c=b1c, b1kv=b1kv,
                 W2=W2, b2bc=b2bc, rwT=rwT,
                 rg=np.ascontiguousarray(rg[c]), ru=np.ascontiguousarray(ru[c]),
                 rd=np.ascontiguousarray(rd[c]), sg=sg, su=su, sd=sd,
                 csel=csel, prevmask=pm, **consts)
        in_maps.append(m)
    res = run_bass_kernel_spmd(nc, in_maps, core_ids=list(range(8)))
    if _debug:
        return res
    return np.stack([np.asarray(r["y"], dtype=np.float32) for r in res.results],
                    axis=0)
